# version 09
# baseline (speedup 1.0000x reference)
import numpy as np
import ml_dtypes
import concourse.bass as bass
import concourse.mybir as mybir
from concourse.bass_utils import run_bass_kernel_spmd

F32 = mybir.dt.float32
BF16 = mybir.dt.bfloat16
AF = mybir.ActivationFunctionType
ALU = mybir.AluOpType
AX = mybir.AxisListType
BF = ml_dtypes.bfloat16

D = 2048
NH = 16
DH = 128
R = 64
NEXP = 16
DI = 1408
DSH = 2816
TOK = 2048
T_FULL = 4096
NTH = 2
THTOK = TOK // NTH
NCH = 2
CH = 512
NKT = D // 128
NF_G = DI // 128
NF_S = DSH // 128
SLOTW = 3 * D

PH_ALL = ("t", "v", "s", "q", "g")
DBG_ATTN = False


_ALLOC_N = [0]


class Alloc:

    def __init__(self, nc):
        self.nc = nc
        self.cms = []
        _ALLOC_N[0] += 1
        self.pfx = "z%d_" % _ALLOC_N[0]

    def sb(self, name, shape, dtype):
        cm = self.nc.sbuf_tensor(self.pfx + name, list(shape), dtype)
        t = cm.__enter__()
        self.cms.append(cm)
        return t

    def ps(self, name, shape, dtype=F32):
        cm = self.nc.psum_tensor(self.pfx + name, list(shape), dtype)
        t = cm.__enter__()
        self.cms.append(cm)
        return t

    def close(self):
        for cm in reversed(self.cms):
            cm.__exit__(None, None, None)
        self.cms = []


class Sem:
    def __init__(self, nc, name, step=1):
        self._cm = nc.semaphore(name)
        self.h = self._cm.__enter__()
        self.step = step
        self.n = 0


class Plans:

    def __init__(self, env):
        self.p = {k: [] for k in PH_ALL}
        self.env = env

    def add(self, eng, fn):
        self.p[eng].append(fn)

    def wait(self, eng, sem, val):
        if val > 0:
            self.p[eng].append(lambda e, s=sem.h, v=val: e.wait_ge(s, v))

    def op(self, eng, sem, fn, guard=None):
        if guard is None:
            guard = eng in ("v", "s")
        if guard and sem.n > 0:
            self.wait(eng, sem, sem.n)
        sem.n += sem.step
        self.p[eng].append(lambda e, f=fn, s=sem.h, st=sem.step: f(e).then_inc(s, st))
        return sem.n

    def emit(self, nc, name):
        _ALLOC_N[0] += 1
        with nc.Block("%s_%d" % (name, _ALLOC_N[0])) as blk:
            blk.tensor(lambda e: [f(e) for f in self.p["t"]])
            blk.vector(lambda e: [f(e) for f in self.p["v"]])
            blk.scalar(lambda e: [f(e) for f in self.p["s"]])
            blk.sync(lambda e: [f(e) for f in self.p["q"]])
            blk.gpsimd(lambda e: [f(e) for f in self.p["g"]])


def mk_inline(nc):
    consts = {}
    consts["id_bf"] = nc.inline_tensor(np.eye(128, dtype=BF), "id_bf").ap()
    consts["id_f32"] = nc.inline_tensor(np.eye(128, dtype=np.float32), "id_f32").ap()
    perm = np.zeros((128, 128), dtype=BF)
    for m in range(128):
        perm[m ^ 1, m] = 1.0
    consts["perm"] = nc.inline_tensor(perm, "perm").ap()
    consts["ones_col_bf"] = nc.inline_tensor(np.ones((128, 1), dtype=BF), "ones_col_bf").ap()
    consts["ones_col_f32"] = nc.inline_tensor(np.ones((128, 1), dtype=np.float32), "ones_col_f32").ap()
    consts["ones_row_f32"] = nc.inline_tensor(np.ones((1, 128), dtype=np.float32), "ones_row_f32").ap()
    idx_eps = np.tile((np.arange(16, dtype=np.float32) * np.float32(1e-6))[None, :], (128, 1))
    consts["idx_eps"] = nc.inline_tensor(idx_eps, "idx_eps").ap()
    sel = np.zeros((16, 16 * 128), dtype=np.float32)
    for e_ in range(16):
        sel[e_, e_ * 128:(e_ + 1) * 128] = 1.0
    consts["sel16"] = nc.inline_tensor(sel, "sel16").ap()
    return consts


def build_program(n_phase=6):
    nc = bass.Bass("TRN2", target_bir_lowering=False)

    din = {}

    def inp(name, shape, dtype=F32):
        din[name] = nc.dram_tensor(name, list(shape), dtype, kind="ExternalInput").ap()
        return din[name]

    x_fm = inp("x_fm", (NKT, 128, TOK))
    inp("wqkv", (48, 128, D))
    inp("wo", (16, 128, D))
    inp("gpw", (NEXP, NF_G, 128, D))
    inp("upw", (NEXP, NF_G, 128, D))
    inp("dpw", (NEXP, 16, 128, DI))
    inp("sgw", (NF_S, 128, D))
    inp("suw", (NF_S, 128, D))
    inp("sdw", (16, 128, DSH))
    inp("iw1", (4, 128, D))
    inp("iw2", (128, 4))
    inp("gww", (128, 16 * NKT))
    inp("pkT", (128, R))
    inp("pvT", (128, R))
    inp("uoT", (R, 128))
    inp("bias16", (1, 16))
    inp("ropc", (128, TOK))
    inp("rops", (128, TOK))
    inp("lmask", (128, 5 * 640), BF16)

    y1d = nc.dram_tensor("y1d", [NKT, 128, TOK], F32,
                         kind=("ExternalOutput" if n_phase == 2 else "Internal")).ap()
    y2d = nc.dram_tensor("y2d", [NKT, 128, TOK], F32,
                         kind=("ExternalOutput" if n_phase == 5 else "Internal")).ap()
    combd = nc.dram_tensor("combd", [NEXP, TOK], F32,
                           kind=("ExternalOutput" if n_phase == 2 else "Internal")).ap()
    dbgxn = nc.dram_tensor("dbgxn", [128, NKT * THTOK], BF16,
                           kind=("ExternalOutput" if n_phase == 2 else "Internal")).ap()
    dk = "ExternalOutput" if (n_phase == 5 and DBG_ATTN) else "Internal"
    vTd = nc.dram_tensor("vTd", [NH, 128, 16 * 128], BF16, kind="Internal").ap()
    qd = nc.dram_tensor("qd", [NH, 128, TOK], BF16, kind=dk).ap()
    kd = nc.dram_tensor("kd", [NH, 128, TOK], BF16, kind=dk).ap()
    vd = nc.dram_tensor("vd", [NH, 128, TOK], BF16, kind=dk).ap()
    dbgctx = nc.dram_tensor("dbgctx", [128, NH * TOK], BF16, kind=dk).ap()
    dbggate = nc.dram_tensor("dbggate", [1, TOK], F32, kind=dk).ap()
    wcb = [nc.dram_tensor("wcb0", [128, 612352], BF16, kind="Internal").ap(),
           nc.dram_tensor("wcb1", [128, 612352], BF16, kind="Internal").ap()]
    xcho = nc.dram_tensor("xcho", [3072, 2048], BF16, kind="Internal")
    xchi = nc.dram_tensor("xchi", [6144, 2048], BF16, kind="Internal")
    out_fm = nc.dram_tensor("out_fm", [NKT, 128, TOK], F32, kind="ExternalOutput").ap()

    consts = mk_inline(nc)

    din["_dbgctx"] = dbgctx
    din["_dbggate"] = dbggate
    env = dict(nc=nc, consts=consts, din=din, y1d=y1d, y2d=y2d, combd=combd, dbgxn=dbgxn, wcb=wcb,
               qd=qd, kd=kd, vd=vd, vTd=vTd, xcho=xcho, xchi=xchi, out_fm=out_fm)
    for s, step in (("s_wg", 16), ("s_xq", 16), ("s_mm", 1), ("s_sc", 1),
                    ("s_ve", 1), ("s_cc", 1)):
        env[s] = Sem(nc, s, step)

    pers = []

    def sb(name, shape, dtype):
        cm = nc.sbuf_tensor(name, list(shape), dtype)
        t = cm.__enter__()
        pers.append(cm)
        return t

    P = {}
    P["id_bf"] = sb("c_id_bf", (128, 128), BF16)
    P["id_f32"] = sb("c_id_f32", (128, 128), F32)
    P["perm"] = sb("c_perm", (128, 128), BF16)
    P["ones_col_bf"] = sb("c_ones_col_bf", (128, 1), BF16)
    P["ones_col_f32"] = sb("c_ones_col_f32", (128, 1), F32)
    P["ones_row_f32"] = sb("c_ones_row_f32", (1, 128), F32)
    P["idx_eps"] = sb("c_idx_eps", (128, 16), F32)
    P["sel16"] = sb("c_sel16", (16, 16 * 128), F32)
    P["gw"] = sb("c_gw", (128, 16 * NKT), BF16)
    P["iw2"] = sb("c_iw2", (128, 4), BF16)
    P["pkT"] = sb("c_pkT", (128, R), BF16)
    P["pvT"] = sb("c_pvT", (128, R), BF16)
    P["uoT"] = sb("c_uoT", (R, 128), BF16)
    P["bias_bc"] = sb("c_bias_bc", (128, 16), F32)
    P["gate_bc"] = sb("c_gate_bc", (128, TOK), BF16)
    env["P"] = P

    phase_consts(env)
    if n_phase >= 2:
        phase_ffn(env, src=x_fm, dst=y1d)
    if n_phase >= 3:
        phase_qkv(env)
    if n_phase >= 5:
        phase_attn(env)
    if n_phase >= 6:
        phase_ffn(env, src=y2d, dst=out_fm)

    pl = Plans(env)
    pl.wait("q", env["s_xq"], env["s_xq"].n)
    pl.emit(nc, "fin")

    for cm in reversed(pers):
        cm.__exit__(None, None, None)
    return nc


def phase_consts(env):
    nc, C, P, din = env["nc"], env["consts"], env["P"], env["din"]
    s_wg, s_xq, s_mm, s_sc = env["s_wg"], env["s_xq"], env["s_mm"], env["s_sc"]
    pl = Plans(env)

    for dst, src in [
        (P["id_f32"][:], C["id_f32"]),
        (P["ones_row_f32"][:], C["ones_row_f32"]),
        (P["idx_eps"][:], C["idx_eps"]),
        (P["sel16"][:], C["sel16"]),
        (P["id_bf"][:], C["id_bf"]),
        (P["perm"][:], C["perm"]),
        (P["ones_col_bf"][:], C["ones_col_bf"]),
        (P["ones_col_f32"][:], C["ones_col_f32"]),
    ]:
        pl.op("q", s_xq, lambda e, d=dst, s=src: e.dma_start(out=d, in_=s))
    for dst, src in [
        (P["gw"][:], din["gww"]),
        (P["iw2"][:], din["iw2"]),
        (P["pkT"][:], din["pkT"]),
        (P["pvT"][:], din["pvT"]),
        (P["uoT"][:], din["uoT"]),
    ]:
        pl.op("g", s_wg, lambda e, d=dst, s=src: e.dma_start(out=d, in_=s))

    with nc.sbuf_tensor("b16row", [1, 16], F32) as b16row, \
         nc.psum_tensor("pb16", [128, 16], F32) as pb16:
        pl.op("q", s_xq, lambda e: e.dma_start(out=b16row[:], in_=din["bias16"]))
        pl.wait("t", s_xq, s_xq.n)
        mmv = pl.op("t", s_mm, lambda e: e.matmul(pb16[:], P["ones_row_f32"][:],
                                                  b16row[:], start=True, stop=True))
        pl.wait("s", s_mm, mmv)
        pl.op("s", s_sc, lambda e: e.copy(P["bias_bc"][:], pb16[:]))

        pl.wait("q", s_xq, s_xq.n)
        pl.wait("g", s_wg, s_wg.n)
        pl.wait("s", s_sc, s_sc.n)
        pl.emit(nc, "b0")


def phase_ffn(env, src, dst):
    nc = env["nc"]
    NSLOT = 3

    al = Alloc(nc)
    acc = al.sb("f_acc", (128, NKT * THTOK), F32)
    xn = al.sb("f_xn", (128, NKT * THTOK), BF16)
    wring = al.sb("f_wring", (128, NSLOT * SLOTW), BF16)
    gs = al.sb("f_gs", (128, NF_S * CH), BF16)
    us = al.sb("f_us", (128, 4 * CH), BF16)
    comb_sb = al.sb("f_comb_sb", (128, 2 * CH), F32)
    tmpb = al.sb("f_tmpb", (128, CH), BF16)
    t32 = al.sb("f_t32", (128, CH), F32)
    xh2 = al.sb("f_xh", (128, 2 * CH), BF16)
    xsq2 = al.sb("f_xsq", (128, 2 * CH), BF16)
    rows = al.sb("f_rows", (1, 3 * CH), F32)
    lgsb = al.sb("f_lgsb", (16, CH), F32)
    rtile = al.sb("f_rtile", (128, 1152), F32)
    combT = al.sb("f_combT", (16, THTOK), F32)
    pstat = al.ps("f_pstat", (1, 2 * CH))
    pbig = al.ps("f_pbig", (128, 2 * CH))
    pbig2 = al.ps("f_pbig2", (128, 2 * CH))
    paux = al.ps("f_paux", (128, CH))
    ptr = al.ps("f_ptr", (128, CH))
    if True:
        pl = Plans(env)
        B = dict(acc=acc, xn=xn, wring=wring, gs=gs, us=us, comb_sb=comb_sb,
                 tmpb=tmpb, t32=t32, xh2=xh2, xsq2=xsq2, rows=rows, lgsb=lgsb,
                 rtile=rtile, combT=combT, pstat=pstat,
                 pbig=pbig, pbig2=pbig2, paux=paux, ptr=ptr,
                 NSLOT=NSLOT)
        st = dict(grp=0, grp_free={}, evtgt={}, combcp=0, prev_inter=0)
        for th in range(NTH):
            _ffn_th(env, pl, th, src, dst, B, st)

        pl.wait("q", env["s_xq"], env["s_xq"].n)
        pl.wait("g", env["s_wg"], env["s_wg"].n)
        pl.emit(nc, "ffn")
    al.close()


def _ffn_th(env, pl, th, src, dst, B, st):
    nc, P, din = env["nc"], env["P"], env["din"]
    s_wg, s_xq, s_mm, s_sc, s_ve = env["s_wg"], env["s_xq"], env["s_mm"], env["s_sc"], env["s_ve"]
    acc, xn, wring = B["acc"], B["xn"], B["wring"]
    pstat, pbig, pbig2 = B["pstat"], B["pbig"], B["pbig2"]
    NSLOT = B["NSLOT"]
    t0 = th * THTOK
    pga, pgb = pbig[:, 0:CH], pbig[:, CH:2 * CH]

    def accsl(k, ch):
        return acc[:, k * THTOK + ch * CH: k * THTOK + ch * CH + CH]

    def xnsl(k, ch):
        return xn[:, k * THTOK + ch * CH: k * THTOK + ch * CH + CH]

    pl.wait("v", s_mm, s_mm.n)
    pl.wait("s", s_mm, s_mm.n)

    for i in range(4):
        srcap = src[4 * i:4 * i + 4, :, t0:t0 + THTOK].rearrange("f p t -> p f t")
        dstap = acc[:].rearrange("p (f t) -> p f t", f=NKT)[:, 4 * i:4 * i + 4, :]
        pl.op("q", s_xq, lambda e, d=dstap, s=srcap: e.dma_start(out=d, in_=s))
    ld_acc = s_xq.n
    pl.wait("s", s_xq, ld_acc)
    pl.wait("v", s_xq, ld_acc)

    for ch in range(NCH):
        pl.wait("t", s_ve, s_ve.n)
        last_stat_mm = 0
        for k in range(NKT):
            b = k % 2
            if k >= 2:
                pl.wait("s", s_mm, last_stat_mm)
                pl.wait("v", s_mm, last_stat_mm)
            cp_t = pl.op("s", s_sc, lambda e, o=B["xh2"][:, b * CH:(b + 1) * CH],
                         i=accsl(k, ch): e.copy(o, i))
            sq_t = pl.op("v", s_ve, lambda e, o=B["xsq2"][:, b * CH:(b + 1) * CH],
                         i=accsl(k, ch): e.tensor_tensor(o, i, i, ALU.mult))
            pl.wait("t", s_sc, cp_t)
            pl.wait("t", s_ve, sq_t)
            pl.add("t", lambda e, m=B["xh2"][:, b * CH:(b + 1) * CH], k=k:
                   e.matmul(pstat[0:1, 0:CH], P["ones_col_bf"][:], m,
                            start=(k == 0), stop=(k == NKT - 1)))
            last_stat_mm = pl.op("t", s_mm, lambda e, m=B["xsq2"][:, b * CH:(b + 1) * CH], k=k:
                                 e.matmul(pstat[0:1, CH:2 * CH], P["ones_col_bf"][:], m,
                                          start=(k == 0), stop=(k == NKT - 1)))
        rows = B["rows"]
        mean_r = rows[0:1, 0:CH]
        var_r = rows[0:1, CH:2 * CH]
        aux_r = rows[0:1, 2 * CH:3 * CH]
        pl.wait("v", s_mm, last_stat_mm)
        pl.op("v", s_ve, lambda e: e.tensor_scalar_mul(mean_r, pstat[0:1, 0:CH], 1.0 / D))
        pl.op("v", s_ve, lambda e: e.tensor_scalar_mul(var_r, pstat[0:1, CH:2 * CH], 1.0 / D))
        pl.op("v", s_ve, lambda e: e.tensor_tensor(aux_r, mean_r, mean_r, ALU.mult))
        pl.op("v", s_ve, lambda e: e.tensor_tensor(var_r, var_r, aux_r, ALU.subtract))
        vtgt = pl.op("v", s_ve, lambda e: e.tensor_scalar_add(var_r, var_r, 1e-5))
        pl.wait("s", s_ve, vtgt)
        sc_t = pl.op("s", s_sc, lambda e: e.activation(aux_r, var_r, AF.Sqrt))
        pl.wait("v", s_sc, sc_t)
        vtgt = pl.op("v", s_ve, lambda e: e.reciprocal(var_r, aux_r))
        rstd_r = var_r
        pl.wait("t", s_ve, vtgt)
        pl.add("t", lambda e: e.matmul(pga, P["ones_row_f32"][:], mean_r, start=True, stop=True))
        bc_mm = pl.op("t", s_mm, lambda e: e.matmul(pgb, P["ones_row_f32"][:], rstd_r,
                                                    start=True, stop=True))
        pl.wait("v", s_mm, bc_mm)
        for k in range(NKT):
            pl.op("v", s_ve, lambda e, k=k, ch=ch:
                  e.tensor_tensor(B["t32"][:], accsl(k, ch), pga, ALU.subtract))
            pl.op("v", s_ve, lambda e, k=k, ch=ch:
                  e.tensor_tensor(xnsl(k, ch), B["t32"][:], pgb, ALU.mult))
    xn_done = s_ve.n

    lgsb, rtile, combT = B["lgsb"], B["rtile"], B["combT"]
    paux, ptr = B["paux"], B["ptr"]
    plog = paux[0:16, :]
    p16 = ptr[0:16, 128:256]

    def ptp_w(ch, w):
        c = (ch * 4 + w) * 16
        return ptr[:, c:c + 16]

    def comb_slot(ch, w):
        c = 128 + (ch * 4 + w) * 16
        return rtile[:, c:c + 16]

    tpmm = {}
    for ch in range(NCH):
        pl.wait("t", s_ve, xn_done)
        pl.wait("t", s_sc, s_sc.n)
        for k in range(NKT):
            fn = lambda e, k=k, ch=ch: e.matmul(plog, P["gw"][:, k * 16:(k + 1) * 16],
                                                xnsl(k, ch), start=(k == 0), stop=(k == NKT - 1))
            if k == NKT - 1:
                lg_mm = pl.op("t", s_mm, fn)
            else:
                pl.add("t", fn)
        pl.wait("s", s_mm, lg_mm)
        if ch > 0:
            pl.wait("s", s_mm, tpmm[(ch - 1, 3)])
        lg_ev = pl.op("s", s_sc, lambda e: e.copy(lgsb[:], plog))
        pl.wait("t", s_sc, lg_ev)
        for w in range(4):
            tpmm[(ch, w)] = pl.op("t", s_mm, lambda e, ch=ch, w=w:
                e.transpose(ptp_w(ch, w), lgsb[:, w * 128:(w + 1) * 128],
                            P["id_f32"][0:16, 0:16]))
    pl.wait("s", s_ve, xn_done)
    for ch in range(NCH):
        for w in range(4):
            i8 = ch * 4 + w
            pl.wait("s", s_mm, tpmm[(ch, w)])
            sc_ap = rtile[:, 256 + i8 * 16: 256 + i8 * 16 + 16]
            sg_ev = pl.op("s", s_sc, lambda e, ch=ch, w=w, sc_ap=sc_ap:
                          e.activation(sc_ap, ptp_w(ch, w), AF.Sigmoid))
            pl.wait("v", s_sc, sg_ev)
            _routing_window(env, pl, rtile, sc_ap, comb_slot(ch, w),
                            384 + i8 * 96)
    routing_ve = env["s_ve"].n

    def emit_combT():
        pl.wait("t", s_ve, routing_ve)
        cp_prev = 0
        for ch in range(NCH):
            for w in range(4):
                if cp_prev:
                    pl.wait("t", s_sc, cp_prev)
                ct_mm = pl.op("t", s_mm, lambda e, ch=ch, w=w:
                              e.transpose(p16, comb_slot(ch, w), P["id_f32"][:]))
                pl.wait("s", s_mm, ct_mm)
                cp_prev = pl.op("s", s_sc, lambda e, w=w, ch=ch:
                                e.copy(combT[:, ch * CH + w * 128: ch * CH + (w + 1) * 128],
                                       p16))
        return cp_prev

    def emit_comb_bc(e_i):
        for ch in range(NCH):
            pl.wait("t", s_sc, st["combcp"])
            bc = pl.op("t", s_mm, lambda e, e_i=e_i, ch=ch: e.matmul(
                paux[:], P["sel16"][:, e_i * 128:(e_i + 1) * 128],
                combT[:, ch * CH:(ch + 1) * CH], start=True, stop=True))
            pl.wait("s", s_mm, bc)
            pl.wait("s", s_ve, st["prev_inter"])
            st["combcp"] = pl.op("s", s_sc, lambda e, ch=ch:
                                 e.copy(comb_sb[:, ch * CH:(ch + 1) * CH], paux[:]))

    wcb = env["wcb"]
    WCB_HALF = 612352
    pass_i = env.setdefault("ffn_pass", 0)
    env["ffn_pass"] = pass_i + 1
    st["coff"] = 0

    def wcb_slice(off, width):
        if off + width <= WCB_HALF:
            return wcb[0][:, off:off + width]
        return wcb[1][:, off - WCB_HALF:off - WCB_HALF + width]

    def stream_group(srcap, width):
        si = st["grp"]
        slot = si % NSLOT
        if si >= NSLOT:
            for sem, val in st["grp_free"][si - NSLOT]:
                pl.wait("g", sem, val)
        off = st["coff"]
        if off < WCB_HALF and off + width > WCB_HALF:
            off = WCB_HALF
        st["coff"] = off + width
        if pass_i == 0:
            nstr = srcap.shape[0]
            w1 = width // nstr
            sr = srcap.rearrange("f p t -> p f t")
            dr = wring[:, slot * SLOTW: slot * SLOTW + width].rearrange(
                "p (f t) -> p f t", f=nstr)
            tgt = pl.op("g", s_wg, lambda e, d=dr, s=sr: e.dma_start(out=d, in_=s))
        else:
            tgt = pl.op("g", s_wg, lambda e, d=wring[:, slot * SLOTW: slot * SLOTW + width],
                        s=wcb_slice(off, width): e.dma_start(out=d, in_=s))
        st["grp"] += 1
        return si, slot, off, tgt

    def finish_group(si, slot, off, width, last_mm, wtgt):
        frees = [(s_mm, last_mm)]
        if pass_i == 0:
            pl.wait("q", s_wg, wtgt)
            wb = pl.op("q", s_xq, lambda e, s=wring[:, slot * SLOTW: slot * SLOTW + width],
                       d=wcb_slice(off, width): e.dma_start(out=d, in_=s))
            frees.append((s_xq, wb))
        st["grp_free"][si] = frees

    def mm_group(nm, ps_ap, w0, nk, mov_fn, wtgt, pre_waits=()):
        pl.wait("t", s_wg, wtgt)
        if nm in st["evtgt"]:
            pl.wait("t", *st["evtgt"][nm])
        for sem, val in pre_waits:
            pl.wait("t", sem, val)
        for kk in range(nk):
            fn = lambda e, kk=kk, w0=w0: e.matmul(
                ps_ap, wring[:, w0 + kk * 128: w0 + (kk + 1) * 128], mov_fn(kk),
                start=(kk == 0), stop=(kk == nk - 1))
            if kk == nk - 1:
                g = pl.op("t", s_mm, fn)
            else:
                pl.add("t", fn)
        return g

    def chunks_of(lst, n):
        return [lst[i:i + n] for i in range(0, len(lst), n)]

    gpw, upw, dpw = din["gpw"], din["upw"], din["dpw"]
    gs, us, tmpb, comb_sb = B["gs"], B["us"], B["tmpb"], B["comb_sb"]
    urot = [0]

    def unit_ps(i):
        t = pbig if i < 2 else pbig2
        return t[:, (i % 2) * CH:(i % 2 + 1) * CH]

    pl.wait("t", s_ve, xn_done)
    for e_i in range(NEXP):
        if e_i > 0:
            emit_comb_bc(e_i)
        for fos in chunks_of(list(range(NF_G)), 3):
            si, slot, off, wtgt = stream_group(gpw[e_i, fos[0]:fos[-1] + 1],
                                               len(fos) * D)
            for j, fo in enumerate(fos):
                for ch in range(NCH):
                    nm = (fo * 2 + ch) % 4
                    ps = unit_ps(nm)
                    g = mm_group(nm, ps, slot * SLOTW + j * D, NKT,
                                 lambda kk, ch=ch: xnsl(kk, ch), wtgt)
                    pl.wait("s", s_mm, g)
                    ev = pl.op("s", s_sc, lambda e, fo=fo, ch=ch, ps=ps:
                               e.activation(gs[:, (fo * 2 + ch) * CH:(fo * 2 + ch + 1) * CH],
                                            ps, AF.Silu))
                    st["evtgt"][nm] = (s_sc, ev)
            finish_group(si, slot, off, len(fos) * D, g, wtgt)
        if e_i == 0:
            st["combcp"] = emit_combT()
            emit_comb_bc(0)
        for fos in chunks_of(list(range(NF_G)), 3):
            si, slot, off, wtgt = stream_group(upw[e_i, fos[0]:fos[-1] + 1],
                                               len(fos) * D)
            for j, fo in enumerate(fos):
                for ch in range(NCH):
                    nm = (fo * 2 + ch) % 4
                    ps = unit_ps(nm)
                    g = mm_group(nm, ps, slot * SLOTW + j * D, NKT,
                                 lambda kk, ch=ch: xnsl(kk, ch), wtgt)
                    ur = urot[0] % 4
                    urot[0] += 1
                    pl.wait("s", s_mm, g)
                    pl.wait("s", s_ve, st.get("us%d" % ur, 0))
                    ev = pl.op("s", s_sc, lambda e, ps=ps, ur=ur:
                               e.copy(us[:, ur * CH:(ur + 1) * CH], ps))
                    st["evtgt"][nm] = (s_sc, ev)
                    pl.wait("v", s_sc, ev)
                    pl.op("v", s_ve, lambda e, fo=fo, ch=ch, ur=ur:
                          e.tensor_tensor(tmpb[:], gs[:, (fo * 2 + ch) * CH:(fo * 2 + ch + 1) * CH],
                                          us[:, ur * CH:(ur + 1) * CH], ALU.mult))
                    uvt = pl.op("v", s_ve, lambda e, fo=fo, ch=ch:
                                e.tensor_tensor(gs[:, (fo * 2 + ch) * CH:(fo * 2 + ch + 1) * CH],
                                                tmpb[:], comb_sb[:, ch * CH:(ch + 1) * CH],
                                                ALU.mult))
                    st["us%d" % ur] = uvt
            finish_group(si, slot, off, len(fos) * D, g, wtgt)
        inter_done = env["s_ve"].n
        st["prev_inter"] = inter_done
        for fo2s in chunks_of(list(range(16)), 3):
            si, slot, off, wtgt = stream_group(dpw[e_i, fo2s[0]:fo2s[-1] + 1],
                                               len(fo2s) * DI)
            for j, fo2 in enumerate(fo2s):
                for ch in range(NCH):
                    nm = (fo2 * 2 + ch) % 4
                    ps = unit_ps(nm)
                    g = mm_group(nm, ps, slot * SLOTW + j * DI, NF_G,
                                 lambda kk, ch=ch: gs[:, (kk * 2 + ch) * CH:(kk * 2 + ch + 1) * CH],
                                 wtgt, pre_waits=((s_ve, inter_done),))
                    pl.wait("v", s_mm, g)
                    ev = pl.op("v", s_ve, lambda e, fo2=fo2, ps=ps, ch=ch:
                               e.tensor_tensor(accsl(fo2, ch), accsl(fo2, ch), ps, ALU.add))
                    st["evtgt"][nm] = (s_ve, ev)
            finish_group(si, slot, off, len(fo2s) * DI, g, wtgt)

    sgw, suw, sdw = din["sgw"], din["suw"], din["sdw"]
    for kh in range(2):
        for fos in chunks_of(list(range(11)), 3):
            fo_g = [kh * 11 + f for f in fos]
            si, slot, off, wtgt = stream_group(sgw[fo_g[0]:fo_g[-1] + 1], len(fos) * D)
            for j, fo in enumerate(fos):
                for ch in range(NCH):
                    nm = (fo * 2 + ch) % 4
                    ps = unit_ps(nm)
                    g = mm_group(nm, ps, slot * SLOTW + j * D, NKT,
                                 lambda kk, ch=ch: xnsl(kk, ch), wtgt)
                    pl.wait("s", s_mm, g)
                    ev = pl.op("s", s_sc, lambda e, fo=fo, ch=ch, ps=ps:
                               e.activation(gs[:, (fo * 2 + ch) * CH:(fo * 2 + ch + 1) * CH],
                                            ps, AF.Silu))
                    st["evtgt"][nm] = (s_sc, ev)
            finish_group(si, slot, off, len(fos) * D, g, wtgt)
        for fos in chunks_of(list(range(11)), 3):
            fo_g = [kh * 11 + f for f in fos]
            si, slot, off, wtgt = stream_group(suw[fo_g[0]:fo_g[-1] + 1], len(fos) * D)
            for j, fo in enumerate(fos):
                for ch in range(NCH):
                    nm = (fo * 2 + ch) % 4
                    ps = unit_ps(nm)
                    g = mm_group(nm, ps, slot * SLOTW + j * D, NKT,
                                 lambda kk, ch=ch: xnsl(kk, ch), wtgt)
                    ur = urot[0] % 4
                    urot[0] += 1
                    pl.wait("s", s_mm, g)
                    pl.wait("s", s_ve, st.get("us%d" % ur, 0))
                    ev = pl.op("s", s_sc, lambda e, ps=ps, ur=ur:
                               e.activation(us[:, ur * CH:(ur + 1) * CH], ps,
                                            AF.Copy, scale=0.5))
                    st["evtgt"][nm] = (s_sc, ev)
                    pl.wait("v", s_sc, ev)
                    uvt = pl.op("v", s_ve, lambda e, fo=fo, ch=ch, ur=ur:
                                e.tensor_tensor(gs[:, (fo * 2 + ch) * CH:(fo * 2 + ch + 1) * CH],
                                                gs[:, (fo * 2 + ch) * CH:(fo * 2 + ch + 1) * CH],
                                                us[:, ur * CH:(ur + 1) * CH], ALU.mult))
                    st["us%d" % ur] = uvt
            finish_group(si, slot, off, len(fos) * D, g, wtgt)
        sh_inter = env["s_ve"].n
        for fo2s in chunks_of(list(range(16)), 3):
            srcv = sdw[fo2s[0]:fo2s[-1] + 1, :, kh * 1408:(kh + 1) * 1408]
            si, slot, off, wtgt = stream_group(srcv.rearrange("f p t -> f p t") if False
                                               else srcv, len(fo2s) * 1408)
            for j, fo2 in enumerate(fo2s):
                for ch in range(NCH):
                    nm = (fo2 * 2 + ch) % 4
                    ps = unit_ps(nm)
                    g = mm_group(nm, ps, slot * SLOTW + j * 1408, 11,
                                 lambda kk, ch=ch: gs[:, (kk * 2 + ch) * CH:(kk * 2 + ch + 1) * CH],
                                 wtgt, pre_waits=((s_ve, sh_inter),))
                    pl.wait("v", s_mm, g)
                    ev = pl.op("v", s_ve, lambda e, fo2=fo2, ps=ps, ch=ch:
                               e.tensor_tensor(accsl(fo2, ch), accsl(fo2, ch), ps, ALU.add))
                    st["evtgt"][nm] = (s_ve, ev)
            finish_group(si, slot, off, len(fo2s) * 1408, g, wtgt)

    pl.wait("q", s_ve, s_ve.n)
    for i in range(4):
        dstap = dst[4 * i:4 * i + 4, :, t0:t0 + THTOK].rearrange("f p t -> p f t")
        srcap = acc[:].rearrange("p (f t) -> p f t", f=NKT)[:, 4 * i:4 * i + 4, :]
        pl.op("q", s_xq, lambda e, d=dstap, s=srcap: e.dma_start(out=d, in_=s))


def _routing_window(env, pl, rtile, sc_ap, comb_ap, base):
    P = env["P"]
    s_ve = env["s_ve"]
    s4c = rtile[:, base:base + 16]
    gsc = rtile[:, base + 16:base + 20]
    thr = rtile[:, base + 20:base + 21]
    gm = rtile[:, base + 21:base + 25]
    sm = rtile[:, base + 32:base + 48]
    t8 = rtile[:, base + 48:base + 56]
    selc = rtile[:, base + 64:base + 80]
    tmp4 = rtile[:, base + 80:base + 84]
    tmp1 = rtile[:, base + 84:base + 85]
    rs = rtile[:, base + 85:base + 86]

    def V(fn):
        pl.wait("v", s_ve, s_ve.n)
        pl.op("v", s_ve, fn)

    V(lambda e: e.tensor_tensor(s4c, sc_ap, P["bias_bc"][:], ALU.add))
    s4cv = s4c.rearrange("p (g j) -> p g j", j=4)
    slots = [s4cv[:, :, j] for j in range(4)]
    pairs = [(0, 1), (0, 2), (0, 3), (1, 2), (1, 3), (2, 3)]
    for n, (i, j) in enumerate(pairs):
        V(lambda e, i=i, j=j: e.tensor_tensor(tmp4, slots[i], slots[j], ALU.add))
        if n == 0:
            V(lambda e: e.tensor_copy(gsc, tmp4))
        else:
            V(lambda e: e.tensor_tensor(gsc, gsc, tmp4, ALU.max))
    gcol = [gsc[:, j:j + 1] for j in range(4)]
    for n, (i, j) in enumerate(pairs):
        V(lambda e, i=i, j=j: e.tensor_tensor(tmp1, gcol[i], gcol[j], ALU.min))
        if n == 0:
            V(lambda e: e.tensor_copy(thr, tmp1))
        else:
            V(lambda e: e.tensor_tensor(thr, thr, tmp1, ALU.max))
    V(lambda e: e.tensor_scalar(gm, gsc, thr, None, ALU.is_ge))
    smv = sm.rearrange("p (g j) -> p g j", j=4)
    for g in range(4):
        V(lambda e, g=g: e.tensor_copy(smv[:, g, :], gm[:, g:g + 1].to_broadcast([128, 4])))
    V(lambda e: e.tensor_tensor(sm, s4c, sm, ALU.mult))
    V(lambda e: e.tensor_tensor(sm, sm, P["idx_eps"][:], ALU.subtract))
    V(lambda e: e.max(out=t8, in_=sm))
    V(lambda e: e.tensor_scalar(selc, sm, t8[:, 5:6], None, ALU.is_ge))
    V(lambda e: e.tensor_tensor(comb_ap, sc_ap, selc, ALU.mult))
    V(lambda e: e.reduce_sum(rs, comb_ap, axis=AX.X))
    V(lambda e: e.tensor_scalar_add(rs, rs, 1e-20))
    V(lambda e: e.reciprocal(rs, rs))
    V(lambda e: e.tensor_scalar_mul(rs, rs, 1.25))
    V(lambda e: e.tensor_scalar(comb_ap, comb_ap, rs, None, ALU.mult))


def _xch_kr(x, h, pos=0):
    return x[pos * 3072 + h * 64: pos * 3072 + h * 64 + 64, :]


def _xch_vr(x, h, pos=0):
    return x[pos * 3072 + 1024 + h * 64: pos * 3072 + 1024 + h * 64 + 64, :]


def _xch_tail(x, h, which, pos=0):
    base = pos * 3072 + (2048 if which == "k" else 2560)
    v = x[base + h * 32: base + h * 32 + 32, :]
    return v.rearrange("w n -> (w n)").rearrange("(p t) -> p t", t=512)


def _emit_ln(env, pl, B, src_of, dst_of, pga, pgb):
    P = env["P"]
    s_mm, s_sc, s_ve = env["s_mm"], env["s_sc"], env["s_ve"]
    pstat, xh2, xsq2, rows, t32 = B["pstat"], B["xh2"], B["xsq2"], B["rows"], B["t32"]
    pl.wait("t", s_ve, s_ve.n)
    last = 0
    for k in range(NKT):
        b = k % 2
        if k >= 2:
            pl.wait("s", s_mm, last)
            pl.wait("v", s_mm, last)
        cp_t = pl.op("s", s_sc, lambda e, o=xh2[:, b * CH:(b + 1) * CH], i=src_of(k):
                     e.copy(o, i))
        sq_t = pl.op("v", s_ve, lambda e, o=xsq2[:, b * CH:(b + 1) * CH], i=src_of(k):
                     e.tensor_tensor(o, i, i, ALU.mult))
        pl.wait("t", s_sc, cp_t)
        pl.wait("t", s_ve, sq_t)
        pl.add("t", lambda e, m=xh2[:, b * CH:(b + 1) * CH], k=k:
               e.matmul(pstat[0:1, 0:CH], P["ones_col_bf"][:], m,
                        start=(k == 0), stop=(k == NKT - 1)))
        last = pl.op("t", s_mm, lambda e, m=xsq2[:, b * CH:(b + 1) * CH], k=k:
                     e.matmul(pstat[0:1, CH:2 * CH], P["ones_col_bf"][:], m,
                              start=(k == 0), stop=(k == NKT - 1)))
    mean_r = rows[0:1, 0:CH]
    var_r = rows[0:1, CH:2 * CH]
    aux_r = rows[0:1, 2 * CH:3 * CH]
    pl.wait("v", s_mm, last)
    pl.op("v", s_ve, lambda e: e.tensor_scalar_mul(mean_r, pstat[0:1, 0:CH], 1.0 / D))
    pl.op("v", s_ve, lambda e: e.tensor_scalar_mul(var_r, pstat[0:1, CH:2 * CH], 1.0 / D))
    pl.op("v", s_ve, lambda e: e.tensor_tensor(aux_r, mean_r, mean_r, ALU.mult))
    pl.op("v", s_ve, lambda e: e.tensor_tensor(var_r, var_r, aux_r, ALU.subtract))
    vt = pl.op("v", s_ve, lambda e: e.tensor_scalar_add(var_r, var_r, 1e-5))
    pl.wait("s", s_ve, vt)
    sct = pl.op("s", s_sc, lambda e: e.activation(aux_r, var_r, AF.Sqrt))
    pl.wait("v", s_sc, sct)
    vt = pl.op("v", s_ve, lambda e: e.reciprocal(var_r, aux_r))
    pl.wait("t", s_ve, vt)
    pl.wait("t", s_sc, s_sc.n)
    pl.add("t", lambda e: e.matmul(pga, P["ones_row_f32"][:], mean_r, start=True, stop=True))
    bc = pl.op("t", s_mm, lambda e: e.matmul(pgb, P["ones_row_f32"][:], var_r,
                                             start=True, stop=True))
    pl.wait("v", s_mm, bc)
    for k in range(NKT):
        pl.op("v", s_ve, lambda e, k=k: e.tensor_tensor(t32[:], src_of(k), pga, ALU.subtract))
        pl.op("v", s_ve, lambda e, k=k: e.tensor_tensor(dst_of(k), t32[:], pgb, ALU.mult))


def phase_qkv(env):
    nc, P, din = env["nc"], env["P"], env["din"]
    s_wg, s_xq, s_mm, s_sc, s_ve = env["s_wg"], env["s_xq"], env["s_mm"], env["s_sc"], env["s_ve"]
    y1d, qd, kd, vd = env["y1d"], env["qd"], env["kd"], env["vd"]
    xcho = env["xcho"].ap()
    NSL = 4
    KSCL = float(1.0 / np.sqrt(np.float64(128.0)))

    al = Alloc(nc)
    xn2 = al.sb("a_xn2", (128, NKT * TOK), BF16)
    y1c = al.sb("a_y1c", (128, 2 * NKT * CH), F32)
    wring = al.sb("a_wring", (128, NSL * D), BF16)
    cost = al.sb("a_cos", (128, TOK), F32)
    sinm = al.sb("a_sinm", (128, TOK), F32)
    qs = al.sb("a_qs", (128, CH), BF16)
    ropeb = al.sb("a_rope", (128, CH), BF16)
    t32a = al.sb("a_t32a", (128, CH), F32)
    t32b = al.sb("a_t32b", (128, CH), F32)
    r64 = al.sb("a_r64", (64, CH), BF16)
    vts = al.sb("a_vts", (128, CH), BF16)
    hinfo = al.sb("a_hinfo", (128, 4 * CH), BF16)
    xh2 = al.sb("a_xh", (128, 2 * CH), BF16)
    xsq2 = al.sb("a_xsq", (128, 2 * CH), BF16)
    rows = al.sb("a_rows", (1, 3 * CH), F32)
    grow = al.sb("a_grow", (1, TOK), F32)
    pstat = al.ps("a_pstat", (128, 2 * CH))
    pbig = al.ps("a_pbig", (128, 2 * CH))
    prot = al.ps("a_prot", (128, CH))
    pr64 = al.ps("a_pr64", (64, CH))
    pinf = al.ps("a_pinf", (1, CH))
    ptv = al.ps("a_ptv", (128, 2 * CH), BF16)
    if True:
        pl = Plans(env)
        pga, pgb = pbig[:, 0:CH], pbig[:, CH:2 * CH]
        B = dict(pstat=pstat, xh2=xh2, xsq2=xsq2, rows=rows, t32=t32a)

        pl.op("q", s_xq, lambda e: e.dma_start(out=cost[:], in_=din["ropc"]))
        pl.op("q", s_xq, lambda e: e.dma_start(out=sinm[:], in_=din["rops"]))

        y1ck = {}
        xn_ch = {}
        for ch in range(4):
            sl = ch % 2
            srcap = y1d[:, :, ch * CH:(ch + 1) * CH].rearrange("f p t -> p f t")
            dstap = y1c[:, sl * NKT * CH:(sl + 1) * NKT * CH].rearrange(
                "p (f t) -> p f t", f=NKT)
            for sem, val in y1ck.get(ch - 2, []):
                pl.wait("q", sem, val)
            ldt = pl.op("q", s_xq, lambda e, d=dstap, s=srcap: e.dma_start(out=d, in_=s))
            pl.wait("s", s_xq, ldt)
            pl.wait("v", s_xq, ldt)
            _emit_ln(env, pl, B,
                     src_of=lambda k, sl=sl: y1c[:, sl * NKT * CH + k * CH:
                                                 sl * NKT * CH + (k + 1) * CH],
                     dst_of=lambda k, ch=ch: xn2[:, k * TOK + ch * CH: k * TOK + ch * CH + CH],
                     pga=pstat[:, 0:CH], pgb=pstat[:, CH:2 * CH])
            y1ck[ch] = [(s_ve, s_ve.n), (s_sc, s_sc.n)]
            xn_ch[ch] = s_ve.n
        xn_done = s_ve.n

        st = dict(strip=0, strip_mm={}, evtgt={}, qs_dma=0, rope_dma=0,
                  pr64_mm=0, rope_mm=0)

        def stream_strip(srcap):
            si = st["strip"]
            slot = si % NSL
            if si >= NSL:
                pl.wait("g", s_mm, st["strip_mm"][si - NSL])
            tgt = pl.op("g", s_wg, lambda e, d=wring[:, slot * D: slot * D + srcap.shape[-1]],
                        s=srcap: e.dma_start(out=d, in_=s))
            st["strip"] += 1
            return si, slot, tgt

        def psum_wait(nm):
            if nm in st["evtgt"]:
                pl.wait("t", *st["evtgt"][nm])

        def qkv_group(slot, ps, ch):
            for kk in range(NKT):
                fn = lambda e, kk=kk, w0=slot * D, ps=ps, ch=ch: e.matmul(
                    ps, wring[:, w0 + kk * 128: w0 + (kk + 1) * 128],
                    xn2[:, kk * TOK + ch * CH: kk * TOK + ch * CH + CH],
                    start=(kk == 0), stop=(kk == NKT - 1))
                if kk == NKT - 1:
                    g = pl.op("t", s_mm, fn)
                else:
                    pl.add("t", fn)
            return g

        for ft in range(48):
            si, slot, wtgt = stream_strip(din["wqkv"][ft])
            kind = "q" if ft < 16 else ("k" if ft < 32 else "v")
            h = ft % 16
            for ch in range(4):
                if ft == 0:
                    pl.wait("t", s_ve, xn_ch[ch])
                nm = "pga" if (ft * 4 + ch) % 2 == 0 else "pgb"
                ps = pga if nm == "pga" else pgb
                pl.wait("t", s_wg, wtgt)
                psum_wait(nm)
                g = qkv_group(slot, ps, ch)
                st["strip_mm"][si] = g
                pl.wait("s", s_mm, g)
                pl.wait("s", s_xq, st["qs_dma"])
                pl.wait("s", s_mm, st["pr64_mm"])
                pl.wait("s", s_ve, st.get("qs_ve", 0))
                if kind == "v":
                    ev = pl.op("s", s_sc, lambda e, ps=ps: e.copy(qs[:], ps))
                    st["evtgt"][nm] = (s_sc, ev)
                    pl.wait("t", s_sc, ev)
                    pl.wait("t", s_ve, st.get("ptv_ve", 0))
                    for i4 in range(4):
                        fn = lambda e, i4=i4: e.transpose(
                            ptv[:, i4 * 128:(i4 + 1) * 128],
                            qs[:, i4 * 128:(i4 + 1) * 128], P["id_bf"][:])
                        if i4 == 3:
                            tmm = pl.op("t", s_mm, fn)
                        else:
                            pl.add("t", fn)
                    pl.wait("v", s_mm, tmm)
                    pl.wait("v", s_xq, st.get("vts_dma", 0))
                    vev = pl.op("v", s_ve, lambda e: e.tensor_copy(vts[:], ptv[:, 0:CH]))
                    st["ptv_ve"] = vev
                    pl.wait("q", s_ve, vev)
                    st["vts_dma"] = pl.op("q", s_xq, lambda e, h=h, ch=ch:
                                          e.dma_start(out=env["vTd"][h][:, ch * CH:(ch + 1) * CH],
                                                      in_=vts[:]))
                    if ch == 3:
                        st["vts_dma"] = pl.op("q", s_xq, lambda e, h=h:
                                              e.dma_start(out=_xch_tail(xcho, h, "v"), in_=vts[:]))
                    psum_wait("pr64")
                    g2 = pl.op("t", s_mm, lambda e: e.matmul(pr64[:], P["pvT"][:], qs[:],
                                                             start=True, stop=True))
                    st["pr64_mm"] = g2
                    pl.wait("s", s_mm, g2)
                    pl.wait("s", s_xq, st.get("r64_dma", 0))
                    e2 = pl.op("s", s_sc, lambda e: e.copy(r64[:], pr64[:]))
                    st["evtgt"]["pr64"] = (s_sc, e2)
                    pl.wait("q", s_sc, e2)
                    st["r64_dma"] = pl.op("q", s_xq, lambda e, h=h, ch=ch:
                                          e.dma_start(out=_xch_vr(xcho, h)[:, ch * CH:(ch + 1) * CH],
                                                      in_=r64[:]))
                else:
                    scl = 1.0 if kind == "q" else KSCL
                    ev = pl.op("s", s_sc, lambda e, ps=ps, scl=scl:
                               e.activation(qs[:], ps, AF.Copy, scale=scl))
                    st["evtgt"][nm] = (s_sc, ev)
                    pl.wait("t", s_sc, ev)
                    psum_wait("prot")
                    rmm = pl.op("t", s_mm, lambda e: e.matmul(prot[:], P["perm"][:], qs[:],
                                                              start=True, stop=True))
                    pl.wait("v", s_mm, rmm)
                    pl.wait("v", s_xq, st["rope_dma"])
                    pl.wait("v", s_mm, st["rope_mm"])
                    st["qs_ve"] = pl.op("v", s_ve, lambda e, ch=ch:
                          e.tensor_tensor(t32a[:], qs[:], cost[:, ch * CH:(ch + 1) * CH],
                                          ALU.mult))
                    vt = pl.op("v", s_ve, lambda e, ch=ch:
                               e.tensor_tensor(t32b[:], prot[:], sinm[:, ch * CH:(ch + 1) * CH],
                                               ALU.mult))
                    st["evtgt"]["prot"] = (s_ve, vt)
                    vt = pl.op("v", s_ve, lambda e: e.tensor_tensor(ropeb[:], t32a[:], t32b[:],
                                                                    ALU.add))
                    dst0 = qd if kind == "q" else kd
                    pl.wait("q", s_ve, vt)
                    st["rope_dma"] = pl.op("q", s_xq, lambda e, d=dst0, h=h, ch=ch:
                                           e.dma_start(out=d[h, :, ch * CH:(ch + 1) * CH],
                                                       in_=ropeb[:]))
                    if kind == "k" and ch == 3:
                        st["rope_dma"] = pl.op("q", s_xq, lambda e, h=h:
                                               e.dma_start(out=_xch_tail(xcho, h, "k"),
                                                           in_=ropeb[:]))
                    if kind == "k":
                        pl.wait("t", s_ve, vt)
                        psum_wait("pr64")
                        g2 = pl.op("t", s_mm, lambda e: e.matmul(pr64[:], P["pkT"][:], ropeb[:],
                                                                 start=True, stop=True))
                        st["rope_mm"] = g2
                        pl.wait("s", s_mm, g2)
                        pl.wait("s", s_xq, st.get("r64_dma", 0))
                        e2 = pl.op("s", s_sc, lambda e: e.copy(r64[:], pr64[:]))
                        st["evtgt"]["pr64"] = (s_sc, e2)
                        pl.wait("q", s_sc, e2)
                        st["r64_dma"] = pl.op("q", s_xq, lambda e, h=h, ch=ch:
                                              e.dma_start(
                                                  out=_xch_kr(xcho, h)[:, ch * CH:(ch + 1) * CH],
                                                  in_=r64[:]))

        pl.wait("g", s_xq, s_xq.n)
        pl.op("g", env["s_cc"], lambda e: e.collective_compute(
            "AllGather", ALU.bypass,
            replica_groups=[[0, 1], [2, 3], [4, 5], [6, 7]],
            ins=[env["xcho"].ap().opt()], outs=[env["xchi"].ap().opt()]),
            guard=False)

        for ch in range(4):
            for fi in range(4):
                si, slot, wtgt = stream_strip(din["iw1"][fi])
                nm = "pga" if fi % 2 == 0 else "pgb"
                ps = pga if nm == "pga" else pgb
                pl.wait("t", s_wg, wtgt)
                psum_wait(nm)
                g = qkv_group(slot, ps, ch)
                st["strip_mm"][si] = g
                pl.wait("s", s_mm, g)
                ev = pl.op("s", s_sc, lambda e, fi=fi, ps=ps:
                           e.activation(hinfo[:, fi * CH:(fi + 1) * CH], ps, AF.Gelu))
                st["evtgt"][nm] = (s_sc, ev)
            pl.wait("t", s_sc, s_sc.n)
            pl.wait("t", s_ve, s_ve.n)
            for kk in range(4):
                fn = lambda e, kk=kk: e.matmul(pinf[:], P["iw2"][:, kk:kk + 1],
                                               hinfo[:, kk * CH:(kk + 1) * CH],
                                               start=(kk == 0), stop=(kk == 3))
                if kk == 3:
                    g = pl.op("t", s_mm, fn)
                else:
                    pl.add("t", fn)
            pl.wait("s", s_mm, g)
            sct = pl.op("s", s_sc, lambda e:
                        e.activation(rows[0:1, 0:CH], pinf[0:1, :], AF.Sigmoid))
            pl.wait("v", s_sc, sct)
            pl.op("v", s_ve, lambda e, ch=ch:
                  e.tensor_single_scalar(grow[0:1, ch * CH:(ch + 1) * CH],
                                         rows[0:1, 0:CH], 0.75, ALU.is_gt))
        if DBG_ATTN:
            pl.wait("q", s_ve, s_ve.n)
            pl.op("q", s_xq, lambda e: e.dma_start(out=din["_dbggate"], in_=grow[:]))
        for ch in range(4):
            pl.wait("t", s_ve, s_ve.n)
            pl.wait("t", s_sc, s_sc.n)
            g = pl.op("t", s_mm, lambda e, ch=ch:
                      e.matmul(pga, P["ones_row_f32"][:], grow[0:1, ch * CH:(ch + 1) * CH],
                               start=True, stop=True))
            pl.wait("s", s_mm, g)
            pl.op("s", s_sc, lambda e, ch=ch:
                  e.copy(P["gate_bc"][:, ch * CH:(ch + 1) * CH], pga))

        pl.wait("q", s_xq, s_xq.n)
        pl.wait("q", s_sc, s_sc.n)
        pl.wait("g", s_wg, s_wg.n)
        pl.emit(nc, "qkv")
    al.close()


def phase_exchange(env):
    nc = env["nc"]
    s_cc = env["s_cc"]
    pl = Plans(env)

    def cc(e):
        return e.collective_compute(
            "AllGather", ALU.bypass,
            replica_groups=[[0, 1], [2, 3], [4, 5], [6, 7]],
            ins=[env["xcho"].ap().opt()], outs=[env["xchi"].ap().opt()])
    pl.op("g", s_cc, cc, guard=False)
    pl.wait("g", s_cc, s_cc.n)
    pl.emit(nc, "xch")


def phase_attn(env):
    nc, P, din = env["nc"], env["P"], env["din"]
    s_wg, s_xq, s_mm, s_sc, s_ve = env["s_wg"], env["s_xq"], env["s_mm"], env["s_sc"], env["s_ve"]
    y1d, y2d = env["y1d"], env["y2d"]
    NSL = 4

    al0 = Alloc(nc)
    ctxall = al0.sb("h_ctxall", (128, NH * TOK), BF16)
    lmask = al0.sb("h_lmask", (128, 5 * 640), BF16)
    if True:
        al = Alloc(nc)
        B = dict(ctxall=ctxall, lmask=lmask, hstate={},
                 reg_free={r: [] for r in range(4)},
                 pR_free={0: [], 1: []}, pG_free={0: [], 1: [], 2: []})
        for par in range(2):
            B["qh%d" % par] = al.sb("h_q%d" % par, (128, TOK), BF16)
            B["kloc%d" % par] = al.sb("h_kloc%d" % par, (128, 2560), BF16)
            B["vT%d" % par] = al.sb("h_vT%d" % par, (128, 2560), BF16)
            B["krT%d" % par] = al.sb("h_krT%d" % par, (64, 2 * TOK), BF16)
            B["vu%d" % par] = al.sb("h_vu%d" % par, (128, 2 * TOK), BF16)
            B["qrT%d" % par] = al.sb("h_qrT%d" % par, (64, TOK), BF16)
        B["vrT"] = al.sb("h_vrT", (64, 2 * TOK), BF16)
        B["probR"] = al.sb("h_probR", (128, 2 * 640), BF16)
        B["probG"] = al.sb("h_probG", (128, 3 * TOK), BF16)
        B["denP"] = al.sb("h_denP", (128, TOK), F32)
        B["ctxL"] = al.sb("h_ctxL", (128, TOK), BF16)
        B["t32"] = al.sb("h_t32", (128, 4 * CH), F32)
        B["t32b"] = al.sb("h_t32b", (128, CH), F32)
        B["rows"] = al.sb("h_rows", (1, TOK), F32)
        B["psA"] = al.ps("h_psA", (128, 2 * CH))
        B["psB"] = al.ps("h_psB", (128, 2 * CH))
        B["pctx"] = al.ps("h_pctx", (128, 4 * CH))
        if True:
            pl = Plans(env)
            pl.op("q", s_xq, lambda e: e.dma_start(out=lmask[:], in_=din["lmask"]))
            pl.wait("v", s_xq, s_xq.n)
            pl.wait("q", env["s_cc"], env["s_cc"].n)
            for h in range(NH):
                _attn_head(env, pl, h, B)
            pl.wait("q", s_xq, s_xq.n)
            pl.wait("g", s_wg, s_wg.n)
            pl.emit(nc, "heads")
        al.close()

        al = Alloc(nc)
        wring = al.sb("h_wring", (128, NSL * D), BF16)
        y1c = al.sb("h_y1c", (128, 2 * NKT * CH), F32)
        pw = al.ps("h_pw", (128, 2 * CH))
        if True:
            pl = Plans(env)
            st = dict(strip=0, strip_mm={}, evtgt={})
            stores = {}

            def issue_load(ch):
                if ch >= 4:
                    return 0
                sl = ch % 2
                src0 = y1d[:, :, ch * CH:(ch + 1) * CH].rearrange("f p t -> p f t")
                dst0 = y1c[:, sl * NKT * CH:(sl + 1) * NKT * CH].rearrange(
                    "p (f t) -> p f t", f=NKT)
                for sem, val in stores.get(ch - 2, []):
                    pl.wait("q", sem, val)
                return pl.op("q", s_xq, lambda e, d=dst0, s=src0:
                             e.dma_start(out=d, in_=s))

            lds = {0: issue_load(0), 1: issue_load(1)}
            for ch in range(4):
                sl = ch % 2
                y0 = sl * NKT * CH
                pl.wait("v", s_xq, lds[ch])
                for ft in range(16):
                    si = st["strip"]
                    slot = si % NSL
                    if si >= NSL:
                        pl.wait("g", s_mm, st["strip_mm"][si - NSL])
                    wtgt = pl.op("g", s_wg, lambda e, ft=ft, slot=slot:
                                 e.dma_start(out=wring[:, slot * D:(slot + 1) * D],
                                             in_=din["wo"][ft]))
                    st["strip"] += 1
                    nm = ft % 2
                    ps = pw[:, nm * CH:(nm + 1) * CH]
                    pl.wait("t", s_wg, wtgt)
                    if nm in st["evtgt"]:
                        pl.wait("t", *st["evtgt"][nm])
                    for kh in range(16):
                        fn = lambda e, kh=kh, slot=slot, ps=ps, ch=ch: e.matmul(
                            ps, wring[:, slot * D + kh * 128: slot * D + (kh + 1) * 128],
                            ctxall[:, kh * TOK + ch * CH: kh * TOK + ch * CH + CH],
                            start=(kh == 0), stop=(kh == 15))
                        if kh == 15:
                            g = pl.op("t", s_mm, fn)
                        else:
                            pl.add("t", fn)
                    st["strip_mm"][si] = g
                    pl.wait("v", s_mm, g)
                    ev = pl.op("v", s_ve, lambda e, ft=ft, ps=ps, y0=y0:
                               e.tensor_tensor(y1c[:, y0 + ft * CH:y0 + (ft + 1) * CH],
                                               y1c[:, y0 + ft * CH:y0 + (ft + 1) * CH],
                                               ps, ALU.add))
                    st["evtgt"][nm] = (s_ve, ev)
                pl.wait("q", s_ve, s_ve.n)
                dst1 = y2d[:, :, ch * CH:(ch + 1) * CH].rearrange("f p t -> p f t")
                src1 = y1c[:, y0:y0 + NKT * CH].rearrange("p (f t) -> p f t", f=NKT)
                stgt = pl.op("q", s_xq, lambda e, d=dst1, s=src1: e.dma_start(out=d, in_=s))
                stores[ch] = [(s_xq, stgt)]
                lds[ch + 2] = issue_load(ch + 2)
            pl.wait("q", s_xq, s_xq.n)
            pl.wait("g", s_wg, s_wg.n)
            pl.emit(nc, "wo")
        al.close()
    al0.close()


def _attn_head(env, pl, h, B):
    nc, P = env["nc"], env["P"]
    s_wg, s_xq, s_mm, s_sc, s_ve = (env["s_wg"], env["s_xq"], env["s_mm"],
                                    env["s_sc"], env["s_ve"])
    qd, kd, vd = env["qd"], env["kd"], env["vd"]
    xchi = env["xchi"].ap()
    par = h % 2
    qh, kloc, vT = B["qh%d" % par], B["kloc%d" % par], B["vT%d" % par]
    krT, vu, qrT = B["krT%d" % par], B["vu%d" % par], B["qrT%d" % par]
    vrT = B["vrT"]
    probR, probG, denP, ctxL = B["probR"], B["probG"], B["denP"], B["ctxL"]
    t32, t32b, rows = B["t32"], B["t32b"], B["rows"]
    psA, psB, pctx = B["psA"], B["psB"], B["pctx"]
    ctxall, lmask = B["ctxall"], B["lmask"]
    reg_free, pR_free, pG_free = B["reg_free"], B["pR_free"], B["pG_free"]

    def reg(r):
        t = psA if r % 2 == 0 else psB
        return t[:, (r // 2) * CH:(r // 2 + 1) * CH]

    if "vrT_mm" in B:
        pl.wait("q", s_mm, B["vrT_mm"])
    for d, s in [(vrT[:, 0:TOK], _xch_vr(xchi, h, 0)),
                 (vrT[:, TOK:2 * TOK], _xch_vr(xchi, h, 1))]:
        pl.op("q", s_xq, lambda e, d=d, s=s: e.dma_start(out=d, in_=s))
    ld_vr = s_xq.n
    for sem, val in B["hstate"].get(par, []):
        pl.wait("q", sem, val)
    for d, s in [
        (qh[:], qd[h]),
        (kloc[:, 512:2560], kd[h]),
        (kloc[:, 0:512], _xch_tail(xchi, h, "k", 0)),
        (krT[:, 0:TOK], _xch_kr(xchi, h, 0)),
        (krT[:, TOK:2 * TOK], _xch_kr(xchi, h, 1)),
    ]:
        pl.op("q", s_xq, lambda e, d=d, s=s: e.dma_start(out=d, in_=s))
    ld_k = s_xq.n
    pl.op("q", s_xq, lambda e: e.dma_start(
        out=vT[:, 0:512], in_=_xch_tail(xchi, h, "v", 0)))
    pl.op("q", s_xq, lambda e: e.dma_start(
        out=vT[:, 512:2560], in_=env["vTd"][h]))
    ld_vt = s_xq.n

    pl.wait("v", s_mm, B.get("denP_mm", 0))
    pl.op("v", s_ve, lambda e: e.memset(denP[:], 0.0))

    pl.wait("t", s_xq, ld_vr)
    for u in range(8):
        r = u % 4
        for sem, val in reg_free[r]:
            pl.wait("t", sem, val)
        for jj in range(4):
            kt = u * 4 + jj
            fn = lambda e, kt=kt, jj=jj, r=r: e.matmul(
                reg(r)[:, jj * 128:(jj + 1) * 128], vrT[:, kt * 128:(kt + 1) * 128],
                P["uoT"][:], start=True, stop=True)
            if jj == 3:
                g = pl.op("t", s_mm, fn)
            else:
                pl.add("t", fn)
        pl.wait("v", s_mm, g)
        cv = pl.op("v", s_ve, lambda e, u=u, r=r:
                   e.tensor_copy(vu[:, u * 512:(u + 1) * 512], reg(r)))
        reg_free[r] = [(s_ve, cv)]
    B["vrT_mm"] = s_mm.n

    pl.wait("t", s_xq, ld_k)
    for ch in range(4):
        r = ch % 4
        for sem, val in reg_free[r]:
            pl.wait("t", sem, val)
        g = pl.op("t", s_mm, lambda e, ch=ch, r=r: e.matmul(
            reg(r)[0:64, :], P["pkT"][:], qh[:, ch * CH:(ch + 1) * CH],
            start=True, stop=True))
        pl.wait("v", s_mm, g)
        qv = pl.op("v", s_ve, lambda e, ch=ch, r=r:
                   e.tensor_scalar_mul(qrT[:, ch * CH:(ch + 1) * CH], reg(r)[0:64, :], 2.0))
        reg_free[r] = [(s_ve, qv)]

    pl.wait("t", s_xq, ld_vt)
    for sem, val in B.get("pctx_free", []):
        pl.wait("t", sem, val)
    ctx_started = [False] * 4

    def issue_lctx(j, vmt):
        k0 = -512 + 128 * j
        lo, hi = max(0, k0), min(TOK, k0 + 640)
        pslot = probR[:, (j % 2) * 640: (j % 2) * 640 + 640]
        pl.wait("t", s_ve, vmt)
        ga = lo
        while ga < hi:
            gb = min(hi, (ga // CH + 1) * CH)
            bank = ga // CH
            stt = not ctx_started[bank]
            ctx_started[bank] = True
            gc = pl.op("t", s_mm, lambda e, j=j, ga=ga, gb=gb, stt=stt, k0=k0,
                       pslot=pslot: e.matmul(
                pctx[:, ga:gb], vT[:, j * 128:(j + 1) * 128],
                pslot[:, ga - k0:gb - k0], start=stt, stop=False))
            ga = gb
        return gc

    pend_l = None
    for j in range(20):
        k0 = -512 + 128 * j
        lo, hi = max(0, k0), min(TOK, k0 + 640)
        c0, c1 = lo - k0, hi - k0
        mbase = (j + 1) * 640 if j < 4 else 0
        unit = j % 2
        ps = psA if unit == 0 else psB
        regs = (0, 2) if unit == 0 else (1, 3)
        slot = j % 2
        pslot = probR[:, slot * 640: slot * 640 + 640]
        for r in regs:
            for sem, val in reg_free[r]:
                pl.wait("t", sem, val)
        a = c0
        while a < c1:
            b2 = min(c1, 512) if a < 512 else c1
            fn = lambda e, j=j, a=a, b2=b2, k0=k0, ps=ps: e.matmul(
                ps[:, a:b2], kloc[:, j * 128:(j + 1) * 128], qh[:, k0 + a:k0 + b2],
                start=True, stop=True)
            if b2 == c1:
                g = pl.op("t", s_mm, fn)
            else:
                pl.add("t", fn)
            a = b2
        pl.wait("s", s_mm, g)
        for sem, val in pR_free[slot]:
            pl.wait("s", sem, val)
        sct = pl.op("s", s_sc, lambda e, c0=c0, c1=c1, ps=ps, pslot=pslot:
                    e.activation(pslot[:, c0:c1], ps[:, c0:c1], AF.Exp))
        pl.wait("v", s_sc, sct)
        vmt = pl.op("v", s_ve, lambda e, c0=c0, c1=c1, mbase=mbase, pslot=pslot:
                    e.tensor_tensor(pslot[:, c0:c1], pslot[:, c0:c1],
                                    lmask[:, mbase:mbase + (c1 - c0)], ALU.mult))
        vda = pl.op("v", s_ve, lambda e, lo=lo, hi=hi, c0=c0, c1=c1, pslot=pslot:
                    e.tensor_tensor(denP[:, lo:hi], denP[:, lo:hi],
                                    pslot[:, c0:c1], ALU.add))
        if pend_l is not None:
            gc = issue_lctx(pend_l[0], pend_l[1])
            pR_free[pend_l[0] % 2] = [(s_ve, pend_l[2]), (s_mm, gc)]
        pend_l = (j, vmt, vda)
        for r in regs:
            reg_free[r] = [(s_sc, sct)]
    gc = issue_lctx(pend_l[0], pend_l[1])
    pR_free[pend_l[0] % 2] = [(s_ve, pend_l[2]), (s_mm, gc)]

    pl.wait("t", s_ve, s_ve.n)
    for r in range(4):
        for sem, val in reg_free[r]:
            pl.wait("t", sem, val)
    dred = {}
    for qq in range(4):
        dred[qq] = pl.op("t", s_mm, lambda e, qq=qq: e.matmul(
            reg(qq)[0:1, :], P["ones_col_f32"][:], denP[:, qq * CH:(qq + 1) * CH],
            start=True, stop=True))
    loc_red_mm = s_mm.n
    pl.wait("v", s_mm, loc_red_mm)
    pl.op("v", s_ve, lambda e: e.memset(denP[:], 0.0))
    l2 = {}
    for qq in range(4):
        pl.wait("s", s_mm, dred[qq])
        pl.op("s", s_sc, lambda e, qq=qq: e.activation(
            rows[0:1, qq * CH:(qq + 1) * CH], reg(qq)[0:1, :], AF.Ln))
        l2[qq] = pl.op("s", s_sc, lambda e, qq=qq: e.activation(
            rows[0:1, qq * CH:(qq + 1) * CH], rows[0:1, qq * CH:(qq + 1) * CH],
            AF.Exp, scale=-1.0))
    bgs = {}
    for qq in range(4):
        pl.wait("t", s_sc, l2[qq])
        bgs[qq] = pl.op("t", s_mm, lambda e, qq=qq: e.matmul(
            reg(qq)[:], P["ones_row_f32"][:], rows[0:1, qq * CH:(qq + 1) * CH],
            start=True, stop=True))
    pl.wait("s", s_ve, B.get("t32_free", 0))
    cps = {}
    for qq in range(4):
        pl.wait("s", s_mm, bgs[qq])
        cps[qq] = pl.op("s", s_sc, lambda e, qq=qq: e.copy(
            t32[:, qq * CH:(qq + 1) * CH], reg(qq)[:]))
        reg_free[qq] = [(s_sc, cps[qq])]
    for qq in range(4):
        pl.wait("v", s_sc, cps[qq])
        pl.op("v", s_ve, lambda e, qq=qq: e.tensor_tensor(
            ctxL[:, qq * CH:(qq + 1) * CH], pctx[:, qq * CH:(qq + 1) * CH],
            t32[:, qq * CH:(qq + 1) * CH], ALU.mult))
    ctxl_done = s_ve.n

    for r in range(4):
        for sem, val in reg_free[r]:
            pl.wait("t", sem, val)
    expA = expB = 0
    exps = {}
    pend = []

    def issue_ctx(kt):
        gsl = probG[:, (kt % 3) * TOK:(kt % 3) * TOK + TOK]
        ea, eb = exps[kt]
        if kt == 0:
            pl.wait("t", s_ve, ctxl_done)
        pl.wait("t", s_sc, ea)
        for qq in range(4):
            if qq == 2:
                pl.wait("t", s_sc, eb)
            fn = lambda e, kt=kt, qq=qq, gsl=gsl: e.matmul(
                pctx[:, qq * CH:(qq + 1) * CH], vu[:, kt * 128:(kt + 1) * 128],
                gsl[:, qq * CH:(qq + 1) * CH], start=(kt == 0), stop=(kt == 31))
            if qq == 3:
                g = pl.op("t", s_mm, fn)
            else:
                pl.add("t", fn)
        return g

    def flush_ctx():
        for (k2, db2) in pend:
            gc = issue_ctx(k2)
            pG_free[k2 % 3] = [(s_ve, db2), (s_mm, gc)]
        del pend[:]

    for kt in range(32):
        slot = kt % 3
        gsl = probG[:, slot * TOK: slot * TOK + TOK]
        if expA:
            pl.wait("t", s_sc, expA)
        for qq in (0, 1):
            fn = lambda e, kt=kt, qq=qq: e.matmul(
                psA[:, qq * CH:(qq + 1) * CH], krT[:, kt * 128:(kt + 1) * 128],
                qrT[:, qq * CH:(qq + 1) * CH], start=True, stop=True)
            if qq == 1:
                gA = pl.op("t", s_mm, fn)
            else:
                pl.add("t", fn)
        pl.wait("s", s_mm, gA)
        for sem, val in pG_free[slot]:
            pl.wait("s", sem, val)
        expA = pl.op("s", s_sc, lambda e, gsl=gsl:
                     e.activation(gsl[:, 0:2 * CH], psA[:], AF.Exp))
        if expB:
            pl.wait("t", s_sc, expB)
        for qq in (2, 3):
            fn = lambda e, kt=kt, qq=qq: e.matmul(
                psB[:, (qq - 2) * CH:(qq - 1) * CH], krT[:, kt * 128:(kt + 1) * 128],
                qrT[:, qq * CH:(qq + 1) * CH], start=True, stop=True)
            if qq == 3:
                gB = pl.op("t", s_mm, fn)
            else:
                pl.add("t", fn)
        pl.wait("s", s_mm, gB)
        expB = pl.op("s", s_sc, lambda e, gsl=gsl:
                     e.activation(gsl[:, 2 * CH:4 * CH], psB[:], AF.Exp))
        exps[kt] = (expA, expB)
        pl.wait("v", s_sc, expA)
        pl.op("v", s_ve, lambda e, gsl=gsl: e.tensor_tensor(
            denP[:, 0:2 * CH], denP[:, 0:2 * CH], gsl[:, 0:2 * CH], ALU.add))
        pl.wait("v", s_sc, expB)
        db = pl.op("v", s_ve, lambda e, gsl=gsl: e.tensor_tensor(
            denP[:, 2 * CH:4 * CH], denP[:, 2 * CH:4 * CH], gsl[:, 2 * CH:4 * CH],
            ALU.add))
        pend.append((kt, db))
        if len(pend) == 2:
            flush_ctx()
    flush_ctx()
    glob_mm = s_mm.n

    pl.wait("t", s_ve, s_ve.n)
    pl.wait("t", s_sc, expB)
    gred = {}
    for qq in range(4):
        gred[qq] = pl.op("t", s_mm, lambda e, qq=qq: e.matmul(
            reg(qq)[0:1, :], P["ones_col_f32"][:], denP[:, qq * CH:(qq + 1) * CH],
            start=True, stop=True))
    B["denP_mm"] = s_mm.n
    l2g = {}
    for qq in range(4):
        pl.wait("s", s_mm, gred[qq])
        pl.op("s", s_sc, lambda e, qq=qq: e.activation(
            rows[0:1, qq * CH:(qq + 1) * CH], reg(qq)[0:1, :], AF.Ln))
        l2g[qq] = pl.op("s", s_sc, lambda e, qq=qq: e.activation(
            rows[0:1, qq * CH:(qq + 1) * CH], rows[0:1, qq * CH:(qq + 1) * CH],
            AF.Exp, scale=-1.0))
    bgs = {}
    for qq in range(4):
        pl.wait("t", s_sc, l2g[qq])
        bgs[qq] = pl.op("t", s_mm, lambda e, qq=qq: e.matmul(
            reg(qq)[:], P["ones_row_f32"][:], rows[0:1, qq * CH:(qq + 1) * CH],
            start=True, stop=True))
    pl.wait("s", s_ve, ctxl_done)
    cps = {}
    for qq in range(4):
        pl.wait("s", s_mm, bgs[qq])
        cps[qq] = pl.op("s", s_sc, lambda e, qq=qq: e.copy(
            t32[:, qq * CH:(qq + 1) * CH], reg(qq)[:]))
        reg_free[qq] = [(s_sc, cps[qq])]
    pl.wait("v", s_mm, glob_mm)
    for qq in range(4):
        pl.wait("v", s_sc, cps[qq])
        pl.op("v", s_ve, lambda e, qq=qq: e.tensor_tensor(
            t32b[:], pctx[:, qq * CH:(qq + 1) * CH], t32[:, qq * CH:(qq + 1) * CH],
            ALU.mult))
        pl.op("v", s_ve, lambda e, qq=qq: e.tensor_tensor(
            t32b[:], t32b[:], P["gate_bc"][:, qq * CH:(qq + 1) * CH], ALU.mult))
        pl.op("v", s_ve, lambda e, qq=qq, h=h: e.tensor_tensor(
            ctxall[:, h * TOK + qq * CH: h * TOK + (qq + 1) * CH],
            t32b[:], ctxL[:, qq * CH:(qq + 1) * CH], ALU.add))
    B["t32_free"] = s_ve.n
    B["pctx_free"] = [(s_ve, s_ve.n)]
    B["hstate"][par] = [(s_mm, s_mm.n)]


def _tile_w(w):
    fo, fi = w.shape
    nft, nk = fo // 128, fi // 128
    return np.ascontiguousarray(
        w.T.reshape(nk, 128, nft, 128).transpose(2, 1, 0, 3).reshape(nft, 128, nk * 128)
    ).astype(np.float32)


def _rope_tables(pos0):
    a = np.arange(0, DH, 2, dtype=np.float32)
    e = a / np.float32(DH)
    b = np.float32(10000.0) ** e
    inv = np.float32(1.0) / b
    t = np.arange(T_FULL, dtype=np.float32)
    freqs = np.einsum("i,j->ij", t, inv).astype(np.float32)
    emb = np.concatenate([freqs, freqs], axis=-1)
    cos = np.cos(emb).astype(np.float32)
    sinm = np.sin(emb).astype(np.float32)
    sinm[:, 0::2] *= np.float32(-1.0)
    return (np.ascontiguousarray(cos[pos0:pos0 + TOK].T),
            np.ascontiguousarray(sinm[pos0:pos0 + TOK].T))


def _lmask(half):
    m = np.zeros((128, 5 * 640), dtype=BF)
    i = np.arange(128)[:, None]
    j = np.arange(640)[None, :]
    m[:, 0:640] = ((j - i >= 0) & (j - i <= 511)).astype(BF)
    if half == 1:
        for jt in range(4):
            w = 128 * (jt + 1)
            blk = (j[:, :w] <= i + 128 * jt - 1).astype(BF)
            m[:, (jt + 1) * 640:(jt + 1) * 640 + w] = blk
    return m


def _host_inputs(L0_attn_x, L0_attn_w_qkv, L0_attn_w_o, L0_attn_pk, L0_attn_pv,
                 L0_attn_uo, L0_attn_info_w1, L0_attn_info_w2, L0_ffn_gate_weight,
                 L0_ffn_e_score_correction_bias, L0_ffn_gate_proj, L0_ffn_up_proj,
                 L0_ffn_down_proj, L0_ffn_shared_gate_w, L0_ffn_shared_up_w,
                 L0_ffn_shared_down_w):
    f32 = lambda a: np.asarray(a, dtype=np.float32)
    shared = {}
    shared["wqkv"] = _tile_w(f32(L0_attn_w_qkv))
    shared["wo"] = _tile_w(f32(L0_attn_w_o))
    shared["gpw"] = np.stack([_tile_w(f32(L0_ffn_gate_proj[e])) for e in range(NEXP)])
    shared["upw"] = np.stack([_tile_w(f32(L0_ffn_up_proj[e])) for e in range(NEXP)])
    shared["dpw"] = np.stack([_tile_w(f32(L0_ffn_down_proj[e])) for e in range(NEXP)])
    shared["sgw"] = _tile_w(f32(L0_ffn_shared_gate_w))
    shared["suw"] = _tile_w(f32(L0_ffn_shared_up_w))
    shared["sdw"] = _tile_w(f32(L0_ffn_shared_down_w))
    shared["iw1"] = _tile_w(f32(L0_attn_info_w1))
    shared["iw2"] = np.ascontiguousarray(f32(L0_attn_info_w2).T.reshape(4, 128).T)
    gwT = f32(L0_ffn_gate_weight).T
    shared["gww"] = np.ascontiguousarray(
        gwT.reshape(NKT, 128, 16).transpose(1, 0, 2).reshape(128, NKT * 16))
    shared["pkT"] = np.ascontiguousarray(f32(L0_attn_pk).T)
    shared["pvT"] = np.ascontiguousarray(f32(L0_attn_pv).T)
    shared["uoT"] = np.ascontiguousarray(f32(L0_attn_uo).T)
    shared["bias16"] = f32(L0_ffn_e_score_correction_bias).reshape(1, 16)

    x = f32(L0_attn_x)
    in_maps = []
    for c in range(8):
        b, h = c // 2, c % 2
        m = dict(shared)
        xs = np.ascontiguousarray(x[b, h * TOK:(h + 1) * TOK].T)
        m["x_fm"] = np.ascontiguousarray(xs.reshape(NKT, 128, TOK))
        cc, ss = _rope_tables(h * TOK)
        m["ropc"] = cc
        m["rops"] = ss
        m["lmask"] = _lmask(h)
        in_maps.append(m)
    return in_maps


def _run(inputs, n_phase=6):
    in_maps = _host_inputs(**inputs)
    nc = build_program(n_phase=n_phase)
    return run_bass_kernel_spmd(nc, in_maps, core_ids=list(range(8)))


def kernel(**inputs):
    res = _run(inputs, 6)
    out = np.empty((4, T_FULL, D), dtype=np.float32)
    for c in range(8):
        b, h = c // 2, c % 2
        o = res.results[c]["out_fm"].reshape(D, TOK)
        out[b, h * TOK:(h + 1) * TOK] = o.T
    return out


# version 11 (final)
# speedup vs baseline: 1.0825x; 1.0825x over previous
import numpy as np
import ml_dtypes
import concourse.bass as bass
import concourse.mybir as mybir
from concourse.bass_utils import run_bass_kernel_spmd

F32 = mybir.dt.float32
BF16 = mybir.dt.bfloat16
AF = mybir.ActivationFunctionType
ALU = mybir.AluOpType
AX = mybir.AxisListType
BF = ml_dtypes.bfloat16

D = 2048
NH = 16
DH = 128
R = 64
NEXP = 16
DI = 1408
DSH = 2816
TOK = 2048
T_FULL = 4096
NTH = 2
THTOK = TOK // NTH
NCH = 2
CH = 512
NKT = D // 128
NF_G = DI // 128
NF_S = DSH // 128
SLOTW = 3 * D

PH_ALL = ("t", "v", "s", "q", "g")
DBG_ATTN = False


_ALLOC_N = [0]


class Alloc:

    def __init__(self, nc):
        self.nc = nc
        self.cms = []
        _ALLOC_N[0] += 1
        self.pfx = "z%d_" % _ALLOC_N[0]

    def sb(self, name, shape, dtype):
        cm = self.nc.sbuf_tensor(self.pfx + name, list(shape), dtype)
        t = cm.__enter__()
        self.cms.append(cm)
        return t

    def ps(self, name, shape, dtype=F32):
        cm = self.nc.psum_tensor(self.pfx + name, list(shape), dtype)
        t = cm.__enter__()
        self.cms.append(cm)
        return t

    def close(self):
        for cm in reversed(self.cms):
            cm.__exit__(None, None, None)
        self.cms = []


class Sem:
    def __init__(self, nc, name, step=1):
        self._cm = nc.semaphore(name)
        self.h = self._cm.__enter__()
        self.step = step
        self.n = 0


class Plans:

    def __init__(self, env):
        self.p = {k: [] for k in PH_ALL}
        self.env = env

    def add(self, eng, fn):
        self.p[eng].append(fn)

    def wait(self, eng, sem, val):
        if val > 0:
            self.p[eng].append(lambda e, s=sem.h, v=val: e.wait_ge(s, v))

    def op(self, eng, sem, fn, guard=None):
        if guard is None:
            guard = eng in ("v", "s")
        if guard and sem.n > 0:
            self.wait(eng, sem, sem.n)
        sem.n += sem.step
        self.p[eng].append(lambda e, f=fn, s=sem.h, st=sem.step: f(e).then_inc(s, st))
        return sem.n

    def emit(self, nc, name):
        _ALLOC_N[0] += 1
        with nc.Block("%s_%d" % (name, _ALLOC_N[0])) as blk:
            blk.tensor(lambda e: [f(e) for f in self.p["t"]])
            blk.vector(lambda e: [f(e) for f in self.p["v"]])
            blk.scalar(lambda e: [f(e) for f in self.p["s"]])
            blk.sync(lambda e: [f(e) for f in self.p["q"]])
            blk.gpsimd(lambda e: [f(e) for f in self.p["g"]])


def mk_inline(nc):
    consts = {}
    consts["id_bf"] = nc.inline_tensor(np.eye(128, dtype=BF), "id_bf").ap()
    consts["id_f32"] = nc.inline_tensor(np.eye(128, dtype=np.float32), "id_f32").ap()
    perm = np.zeros((128, 128), dtype=BF)
    for m in range(128):
        perm[m ^ 1, m] = 1.0
    consts["perm"] = nc.inline_tensor(perm, "perm").ap()
    consts["ones_col_bf"] = nc.inline_tensor(np.ones((128, 1), dtype=BF), "ones_col_bf").ap()
    consts["ones_col_f32"] = nc.inline_tensor(np.ones((128, 1), dtype=np.float32), "ones_col_f32").ap()
    consts["ones_row_f32"] = nc.inline_tensor(np.ones((1, 128), dtype=np.float32), "ones_row_f32").ap()
    idx_eps = np.tile((np.arange(16, dtype=np.float32) * np.float32(1e-6))[None, :], (128, 1))
    consts["idx_eps"] = nc.inline_tensor(idx_eps, "idx_eps").ap()
    sel = np.zeros((16, 16 * 128), dtype=np.float32)
    for e_ in range(16):
        sel[e_, e_ * 128:(e_ + 1) * 128] = 1.0
    consts["sel16"] = nc.inline_tensor(sel, "sel16").ap()
    return consts


def build_program(n_phase=6):
    nc = bass.Bass("TRN2", target_bir_lowering=False)

    din = {}

    def inp(name, shape, dtype=F32):
        din[name] = nc.dram_tensor(name, list(shape), dtype, kind="ExternalInput").ap()
        return din[name]

    x_fm = inp("x_fm", (NKT, 128, TOK))
    inp("wqkv", (48, 128, D))
    inp("wo", (16, 128, D))
    inp("gpw", (NEXP, NF_G, 128, D))
    inp("upw", (NEXP, NF_G, 128, D))
    inp("dpw", (NEXP, 16, 128, DI))
    inp("sgw", (NF_S, 128, D))
    inp("suw", (NF_S, 128, D))
    inp("sdw", (16, 128, DSH))
    inp("iw1", (4, 128, D))
    inp("iw2", (128, 4))
    inp("gww", (128, 16 * NKT))
    inp("pkT", (128, R))
    inp("pvT", (128, R))
    inp("uoT", (R, 128))
    inp("bias16", (1, 16))
    inp("ropc", (128, TOK))
    inp("rops", (128, TOK))
    inp("lmask", (128, 5 * 640), BF16)

    y1d = nc.dram_tensor("y1d", [NKT, 128, TOK], F32,
                         kind=("ExternalOutput" if n_phase == 2 else "Internal")).ap()
    y2d = nc.dram_tensor("y2d", [NKT, 128, TOK], F32,
                         kind=("ExternalOutput" if n_phase == 5 else "Internal")).ap()
    combd = nc.dram_tensor("combd", [NEXP, TOK], F32,
                           kind=("ExternalOutput" if n_phase == 2 else "Internal")).ap()
    dbgxn = nc.dram_tensor("dbgxn", [128, NKT * THTOK], BF16,
                           kind=("ExternalOutput" if n_phase == 2 else "Internal")).ap()
    dk = "ExternalOutput" if (n_phase == 5 and DBG_ATTN) else "Internal"
    vTd = nc.dram_tensor("vTd", [NH, 128, 16 * 128], BF16, kind="Internal").ap()
    qd = nc.dram_tensor("qd", [NH, 128, TOK], BF16, kind=dk).ap()
    kd = nc.dram_tensor("kd", [NH, 128, TOK], BF16, kind=dk).ap()
    vd = nc.dram_tensor("vd", [NH, 128, TOK], BF16, kind=dk).ap()
    dbgctx = nc.dram_tensor("dbgctx", [128, NH * TOK], BF16, kind=dk).ap()
    dbggate = nc.dram_tensor("dbggate", [1, TOK], F32, kind=dk).ap()
    wcb = [nc.dram_tensor("wcb0", [128, 612352], BF16, kind="Internal").ap(),
           nc.dram_tensor("wcb1", [128, 612352], BF16, kind="Internal").ap()]
    xcho = nc.dram_tensor("xcho", [3072, 2048], BF16, kind="Internal")
    xchi = nc.dram_tensor("xchi", [6144, 2048], BF16, kind="Internal")
    out_fm = nc.dram_tensor("out_fm", [NKT, 128, TOK], F32, kind="ExternalOutput").ap()

    consts = mk_inline(nc)

    din["_dbgctx"] = dbgctx
    din["_dbggate"] = dbggate
    env = dict(nc=nc, consts=consts, din=din, y1d=y1d, y2d=y2d, combd=combd, dbgxn=dbgxn, wcb=wcb,
               qd=qd, kd=kd, vd=vd, vTd=vTd, xcho=xcho, xchi=xchi, out_fm=out_fm)
    for s, step in (("s_wg", 16), ("s_xq", 16), ("s_mm", 1), ("s_sc", 1),
                    ("s_ve", 1), ("s_cc", 1)):
        env[s] = Sem(nc, s, step)

    pers = []

    def sb(name, shape, dtype):
        cm = nc.sbuf_tensor(name, list(shape), dtype)
        t = cm.__enter__()
        pers.append(cm)
        return t

    P = {}
    P["id_bf"] = sb("c_id_bf", (128, 128), BF16)
    P["id_f32"] = sb("c_id_f32", (128, 128), F32)
    P["perm"] = sb("c_perm", (128, 128), BF16)
    P["ones_col_bf"] = sb("c_ones_col_bf", (128, 1), BF16)
    P["ones_col_f32"] = sb("c_ones_col_f32", (128, 1), F32)
    P["ones_row_f32"] = sb("c_ones_row_f32", (1, 128), F32)
    P["idx_eps"] = sb("c_idx_eps", (128, 16), F32)
    P["sel16"] = sb("c_sel16", (16, 16 * 128), F32)
    P["gw"] = sb("c_gw", (128, 16 * NKT), BF16)
    P["iw2"] = sb("c_iw2", (128, 4), BF16)
    P["pkT"] = sb("c_pkT", (128, R), BF16)
    P["pvT"] = sb("c_pvT", (128, R), BF16)
    P["uoT"] = sb("c_uoT", (R, 128), BF16)
    P["bias_bc"] = sb("c_bias_bc", (128, 16), F32)
    P["gate_bc"] = sb("c_gate_bc", (128, TOK), BF16)
    env["P"] = P

    phase_consts(env)
    if n_phase >= 2:
        phase_ffn(env, src=x_fm, dst=y1d)
    if n_phase >= 3:
        phase_qkv(env)
    if n_phase >= 5:
        phase_attn(env)
    if n_phase >= 6:
        phase_ffn(env, src=y2d, dst=out_fm)

    pl = Plans(env)
    pl.wait("q", env["s_xq"], env["s_xq"].n)
    pl.emit(nc, "fin")

    for cm in reversed(pers):
        cm.__exit__(None, None, None)
    return nc


def phase_consts(env):
    nc, C, P, din = env["nc"], env["consts"], env["P"], env["din"]
    s_wg, s_xq, s_mm, s_sc = env["s_wg"], env["s_xq"], env["s_mm"], env["s_sc"]
    pl = Plans(env)

    for dst, src in [
        (P["id_f32"][:], C["id_f32"]),
        (P["ones_row_f32"][:], C["ones_row_f32"]),
        (P["idx_eps"][:], C["idx_eps"]),
        (P["sel16"][:], C["sel16"]),
        (P["id_bf"][:], C["id_bf"]),
        (P["perm"][:], C["perm"]),
        (P["ones_col_bf"][:], C["ones_col_bf"]),
        (P["ones_col_f32"][:], C["ones_col_f32"]),
    ]:
        pl.op("q", s_xq, lambda e, d=dst, s=src: e.dma_start(out=d, in_=s))
    for dst, src in [
        (P["gw"][:], din["gww"]),
        (P["iw2"][:], din["iw2"]),
        (P["pkT"][:], din["pkT"]),
        (P["pvT"][:], din["pvT"]),
        (P["uoT"][:], din["uoT"]),
    ]:
        pl.op("g", s_wg, lambda e, d=dst, s=src: e.dma_start(out=d, in_=s))

    with nc.sbuf_tensor("b16row", [1, 16], F32) as b16row, \
         nc.psum_tensor("pb16", [128, 16], F32) as pb16:
        pl.op("q", s_xq, lambda e: e.dma_start(out=b16row[:], in_=din["bias16"]))
        pl.wait("t", s_xq, s_xq.n)
        mmv = pl.op("t", s_mm, lambda e: e.matmul(pb16[:], P["ones_row_f32"][:],
                                                  b16row[:], start=True, stop=True))
        pl.wait("s", s_mm, mmv)
        pl.op("s", s_sc, lambda e: e.copy(P["bias_bc"][:], pb16[:]))

        pl.wait("q", s_xq, s_xq.n)
        pl.wait("g", s_wg, s_wg.n)
        pl.wait("s", s_sc, s_sc.n)
        pl.emit(nc, "b0")


def phase_ffn(env, src, dst):
    nc = env["nc"]
    NSLOT = 3

    al = Alloc(nc)
    acc = al.sb("f_acc", (128, NKT * THTOK), F32)
    xn = al.sb("f_xn", (128, NKT * THTOK), BF16)
    wring = al.sb("f_wring", (128, NSLOT * SLOTW), BF16)
    gs = al.sb("f_gs", (128, NF_S * CH), BF16)
    us = al.sb("f_us", (128, 4 * CH), BF16)
    comb_sb = al.sb("f_comb_sb", (128, 2 * CH), F32)
    tmpb = al.sb("f_tmpb", (128, CH), BF16)
    t32 = al.sb("f_t32", (128, CH), F32)
    xh2 = al.sb("f_xh", (128, 2 * CH), BF16)
    xsq2 = al.sb("f_xsq", (128, 2 * CH), BF16)
    rows = al.sb("f_rows", (1, 3 * CH), F32)
    lgsb = al.sb("f_lgsb", (16, CH), F32)
    rtile = al.sb("f_rtile", (128, 944), F32)
    combT = al.sb("f_combT", (16, THTOK), F32)
    pstat = al.ps("f_pstat", (1, 2 * CH))
    pbig = al.ps("f_pbig", (128, 2 * CH))
    pbig2 = al.ps("f_pbig2", (128, 2 * CH))
    paux = al.ps("f_paux", (128, CH))
    ptr = al.ps("f_ptr", (128, CH))
    if True:
        pl = Plans(env)
        B = dict(acc=acc, xn=xn, wring=wring, gs=gs, us=us, comb_sb=comb_sb,
                 tmpb=tmpb, t32=t32, xh2=xh2, xsq2=xsq2, rows=rows, lgsb=lgsb,
                 rtile=rtile, combT=combT, pstat=pstat,
                 pbig=pbig, pbig2=pbig2, paux=paux, ptr=ptr,
                 NSLOT=NSLOT)
        st = dict(grp=0, grp_free={}, evtgt={}, combcp=0, prev_inter=0)
        for th in range(NTH):
            _ffn_th(env, pl, th, src, dst, B, st)

        pl.wait("q", env["s_xq"], env["s_xq"].n)
        pl.wait("g", env["s_wg"], env["s_wg"].n)
        pl.emit(nc, "ffn")
    al.close()


def _ffn_th(env, pl, th, src, dst, B, st):
    nc, P, din = env["nc"], env["P"], env["din"]
    s_wg, s_xq, s_mm, s_sc, s_ve = env["s_wg"], env["s_xq"], env["s_mm"], env["s_sc"], env["s_ve"]
    acc, xn, wring = B["acc"], B["xn"], B["wring"]
    pstat, pbig, pbig2 = B["pstat"], B["pbig"], B["pbig2"]
    NSLOT = B["NSLOT"]
    t0 = th * THTOK
    pga, pgb = pbig[:, 0:CH], pbig[:, CH:2 * CH]

    def accsl(k, ch):
        return acc[:, k * THTOK + ch * CH: k * THTOK + ch * CH + CH]

    def xnsl(k, ch):
        return xn[:, k * THTOK + ch * CH: k * THTOK + ch * CH + CH]

    pl.wait("v", s_mm, s_mm.n)
    pl.wait("s", s_mm, s_mm.n)

    for i in range(4):
        srcap = src[4 * i:4 * i + 4, :, t0:t0 + THTOK].rearrange("f p t -> p f t")
        dstap = acc[:].rearrange("p (f t) -> p f t", f=NKT)[:, 4 * i:4 * i + 4, :]
        pl.op("q", s_xq, lambda e, d=dstap, s=srcap: e.dma_start(out=d, in_=s))
    ld_acc = s_xq.n
    pl.wait("s", s_xq, ld_acc)
    pl.wait("v", s_xq, ld_acc)

    for ch in range(NCH):
        pl.wait("t", s_ve, s_ve.n)
        last_stat_mm = 0
        for k in range(NKT):
            b = k % 2
            if k >= 2:
                pl.wait("s", s_mm, last_stat_mm)
                pl.wait("v", s_mm, last_stat_mm)
            cp_t = pl.op("s", s_sc, lambda e, o=B["xh2"][:, b * CH:(b + 1) * CH],
                         i=accsl(k, ch): e.copy(o, i))
            sq_t = pl.op("v", s_ve, lambda e, o=B["xsq2"][:, b * CH:(b + 1) * CH],
                         i=accsl(k, ch): e.tensor_tensor(o, i, i, ALU.mult))
            pl.wait("t", s_sc, cp_t)
            pl.wait("t", s_ve, sq_t)
            pl.add("t", lambda e, m=B["xh2"][:, b * CH:(b + 1) * CH], k=k:
                   e.matmul(pstat[0:1, 0:CH], P["ones_col_bf"][:], m,
                            start=(k == 0), stop=(k == NKT - 1)))
            last_stat_mm = pl.op("t", s_mm, lambda e, m=B["xsq2"][:, b * CH:(b + 1) * CH], k=k:
                                 e.matmul(pstat[0:1, CH:2 * CH], P["ones_col_bf"][:], m,
                                          start=(k == 0), stop=(k == NKT - 1)))
        rows = B["rows"]
        mean_r = rows[0:1, 0:CH]
        var_r = rows[0:1, CH:2 * CH]
        aux_r = rows[0:1, 2 * CH:3 * CH]
        pl.wait("v", s_mm, last_stat_mm)
        pl.op("v", s_ve, lambda e: e.tensor_scalar_mul(mean_r, pstat[0:1, 0:CH], 1.0 / D))
        pl.op("v", s_ve, lambda e: e.tensor_scalar_mul(var_r, pstat[0:1, CH:2 * CH], 1.0 / D))
        pl.op("v", s_ve, lambda e: e.tensor_tensor(aux_r, mean_r, mean_r, ALU.mult))
        pl.op("v", s_ve, lambda e: e.tensor_tensor(var_r, var_r, aux_r, ALU.subtract))
        vtgt = pl.op("v", s_ve, lambda e: e.tensor_scalar_add(var_r, var_r, 1e-5))
        pl.wait("s", s_ve, vtgt)
        sc_t = pl.op("s", s_sc, lambda e: e.activation(aux_r, var_r, AF.Sqrt))
        pl.wait("v", s_sc, sc_t)
        vtgt = pl.op("v", s_ve, lambda e: e.reciprocal(var_r, aux_r))
        rstd_r = var_r
        pl.wait("t", s_ve, vtgt)
        pl.add("t", lambda e: e.matmul(pga, P["ones_row_f32"][:], mean_r, start=True, stop=True))
        bc_mm = pl.op("t", s_mm, lambda e: e.matmul(pgb, P["ones_row_f32"][:], rstd_r,
                                                    start=True, stop=True))
        pl.wait("v", s_mm, bc_mm)
        for k in range(NKT):
            pl.op("v", s_ve, lambda e, k=k, ch=ch:
                  e.tensor_tensor(B["t32"][:], accsl(k, ch), pga, ALU.subtract))
            pl.op("v", s_ve, lambda e, k=k, ch=ch:
                  e.tensor_tensor(xnsl(k, ch), B["t32"][:], pgb, ALU.mult))
    xn_done = s_ve.n

    lgsb, rtile, combT = B["lgsb"], B["rtile"], B["combT"]
    paux, ptr = B["paux"], B["ptr"]
    plog = paux[0:16, :]
    p16 = ptr[0:16, 128:256]

    def ptp_w(ch, w):
        c = (ch * 4 + w) * 16
        return ptr[:, c:c + 16]

    def comb_slot(ch, w):
        c = (ch * 4 + w) * 16
        return rtile[:, c:c + 16]

    tpmm = {}
    for ch in range(NCH):
        pl.wait("t", s_ve, xn_done)
        pl.wait("t", s_sc, s_sc.n)
        for k in range(NKT):
            fn = lambda e, k=k, ch=ch: e.matmul(plog, P["gw"][:, k * 16:(k + 1) * 16],
                                                xnsl(k, ch), start=(k == 0), stop=(k == NKT - 1))
            if k == NKT - 1:
                lg_mm = pl.op("t", s_mm, fn)
            else:
                pl.add("t", fn)
        pl.wait("s", s_mm, lg_mm)
        if ch > 0:
            pl.wait("s", s_mm, tpmm[(ch - 1, 3)])
        lg_ev = pl.op("s", s_sc, lambda e: e.copy(lgsb[:], plog))
        pl.wait("t", s_sc, lg_ev)
        for w in range(4):
            tpmm[(ch, w)] = pl.op("t", s_mm, lambda e, ch=ch, w=w:
                e.transpose(ptp_w(ch, w), lgsb[:, w * 128:(w + 1) * 128],
                            P["id_f32"][0:16, 0:16]))
    pl.wait("s", s_ve, xn_done)
    for ch in range(NCH):
        for w in range(4):
            i8 = ch * 4 + w
            pl.wait("s", s_mm, tpmm[(ch, w)])
            sc_ap = rtile[:, 128 + i8 * 16: 128 + i8 * 16 + 16]
            sg_ev = pl.op("s", s_sc, lambda e, ch=ch, w=w, sc_ap=sc_ap:
                          e.activation(sc_ap, ptp_w(ch, w), AF.Sigmoid))
            pl.wait("v", s_sc, sg_ev)
            _routing_window(env, pl, rtile, sc_ap, comb_slot(ch, w),
                            256 + i8 * 86)
    routing_ve = env["s_ve"].n

    def emit_combT():
        pl.wait("t", s_ve, routing_ve)
        cp_prev = 0
        for ch in range(NCH):
            for w in range(4):
                if cp_prev:
                    pl.wait("t", s_sc, cp_prev)
                ct_mm = pl.op("t", s_mm, lambda e, ch=ch, w=w:
                              e.transpose(p16, comb_slot(ch, w), P["id_f32"][:]))
                pl.wait("s", s_mm, ct_mm)
                cp_prev = pl.op("s", s_sc, lambda e, w=w, ch=ch:
                                e.copy(combT[:, ch * CH + w * 128: ch * CH + (w + 1) * 128],
                                       p16))
        return cp_prev

    def emit_comb_bc(e_i):
        for ch in range(NCH):
            pl.wait("t", s_sc, st["combcp"])
            bc = pl.op("t", s_mm, lambda e, e_i=e_i, ch=ch: e.matmul(
                paux[:], P["sel16"][:, e_i * 128:(e_i + 1) * 128],
                combT[:, ch * CH:(ch + 1) * CH], start=True, stop=True))
            pl.wait("s", s_mm, bc)
            pl.wait("s", s_ve, st["prev_inter"])
            st["combcp"] = pl.op("s", s_sc, lambda e, ch=ch:
                                 e.copy(comb_sb[:, ch * CH:(ch + 1) * CH], paux[:]))

    wcb = env["wcb"]
    WCB_HALF = 612352
    pass_i = env.setdefault("ffn_pass", 0)
    env["ffn_pass"] = pass_i + 1
    st["coff"] = 0

    def wcb_slice(off, width):
        if off + width <= WCB_HALF:
            return wcb[0][:, off:off + width]
        return wcb[1][:, off - WCB_HALF:off - WCB_HALF + width]

    def stream_group(srcap, width):
        si = st["grp"]
        slot = si % NSLOT
        if si >= NSLOT:
            for sem, val in st["grp_free"][si - NSLOT]:
                pl.wait("g", sem, val)
        off = st["coff"]
        if off < WCB_HALF and off + width > WCB_HALF:
            off = WCB_HALF
        st["coff"] = off + width
        if pass_i == 0:
            nstr = srcap.shape[0]
            w1 = width // nstr
            sr = srcap.rearrange("f p t -> p f t")
            dr = wring[:, slot * SLOTW: slot * SLOTW + width].rearrange(
                "p (f t) -> p f t", f=nstr)
            tgt = pl.op("g", s_wg, lambda e, d=dr, s=sr: e.dma_start(out=d, in_=s))
        else:
            tgt = pl.op("g", s_wg, lambda e, d=wring[:, slot * SLOTW: slot * SLOTW + width],
                        s=wcb_slice(off, width): e.dma_start(out=d, in_=s))
        st["grp"] += 1
        return si, slot, off, tgt

    def finish_group(si, slot, off, width, last_mm, wtgt):
        frees = [(s_mm, last_mm)]
        if pass_i == 0:
            pl.wait("q", s_wg, wtgt)
            wb = pl.op("q", s_xq, lambda e, s=wring[:, slot * SLOTW: slot * SLOTW + width],
                       d=wcb_slice(off, width): e.dma_start(out=d, in_=s))
            frees.append((s_xq, wb))
        st["grp_free"][si] = frees

    def mm_group(nm, ps_ap, w0, nk, mov_fn, wtgt, pre_waits=()):
        pl.wait("t", s_wg, wtgt)
        if nm in st["evtgt"]:
            pl.wait("t", *st["evtgt"][nm])
        for sem, val in pre_waits:
            pl.wait("t", sem, val)
        for kk in range(nk):
            fn = lambda e, kk=kk, w0=w0: e.matmul(
                ps_ap, wring[:, w0 + kk * 128: w0 + (kk + 1) * 128], mov_fn(kk),
                start=(kk == 0), stop=(kk == nk - 1))
            if kk == nk - 1:
                g = pl.op("t", s_mm, fn)
            else:
                pl.add("t", fn)
        return g

    def chunks_of(lst, n):
        return [lst[i:i + n] for i in range(0, len(lst), n)]

    gpw, upw, dpw = din["gpw"], din["upw"], din["dpw"]
    gs, us, tmpb, comb_sb = B["gs"], B["us"], B["tmpb"], B["comb_sb"]
    urot = [0]

    def unit_ps(i):
        t = pbig if i < 2 else pbig2
        return t[:, (i % 2) * CH:(i % 2 + 1) * CH]

    pl.wait("t", s_ve, xn_done)
    for e_i in range(NEXP):
        if e_i > 0:
            emit_comb_bc(e_i)
        for fos in chunks_of(list(range(NF_G)), 3):
            si, slot, off, wtgt = stream_group(gpw[e_i, fos[0]:fos[-1] + 1],
                                               len(fos) * D)
            for j, fo in enumerate(fos):
                for ch in range(NCH):
                    nm = (fo * 2 + ch) % 4
                    ps = unit_ps(nm)
                    g = mm_group(nm, ps, slot * SLOTW + j * D, NKT,
                                 lambda kk, ch=ch: xnsl(kk, ch), wtgt)
                    pl.wait("s", s_mm, g)
                    ev = pl.op("s", s_sc, lambda e, fo=fo, ch=ch, ps=ps:
                               e.activation(gs[:, (fo * 2 + ch) * CH:(fo * 2 + ch + 1) * CH],
                                            ps, AF.Silu))
                    st["evtgt"][nm] = (s_sc, ev)
            finish_group(si, slot, off, len(fos) * D, g, wtgt)
        if e_i == 0:
            st["combcp"] = emit_combT()
            emit_comb_bc(0)
        for fos in chunks_of(list(range(NF_G)), 3):
            si, slot, off, wtgt = stream_group(upw[e_i, fos[0]:fos[-1] + 1],
                                               len(fos) * D)
            for j, fo in enumerate(fos):
                for ch in range(NCH):
                    nm = (fo * 2 + ch) % 4
                    ps = unit_ps(nm)
                    g = mm_group(nm, ps, slot * SLOTW + j * D, NKT,
                                 lambda kk, ch=ch: xnsl(kk, ch), wtgt)
                    ur = urot[0] % 4
                    urot[0] += 1
                    pl.wait("s", s_mm, g)
                    pl.wait("s", s_ve, st.get("us%d" % ur, 0))
                    ev = pl.op("s", s_sc, lambda e, ps=ps, ur=ur:
                               e.copy(us[:, ur * CH:(ur + 1) * CH], ps))
                    st["evtgt"][nm] = (s_sc, ev)
                    pl.wait("v", s_sc, ev)
                    pl.op("v", s_ve, lambda e, fo=fo, ch=ch, ur=ur:
                          e.tensor_tensor(tmpb[:], gs[:, (fo * 2 + ch) * CH:(fo * 2 + ch + 1) * CH],
                                          us[:, ur * CH:(ur + 1) * CH], ALU.mult))
                    uvt = pl.op("v", s_ve, lambda e, fo=fo, ch=ch:
                                e.tensor_tensor(gs[:, (fo * 2 + ch) * CH:(fo * 2 + ch + 1) * CH],
                                                tmpb[:], comb_sb[:, ch * CH:(ch + 1) * CH],
                                                ALU.mult))
                    st["us%d" % ur] = uvt
            finish_group(si, slot, off, len(fos) * D, g, wtgt)
        inter_done = env["s_ve"].n
        st["prev_inter"] = inter_done
        for fo2s in chunks_of(list(range(16)), 3):
            si, slot, off, wtgt = stream_group(dpw[e_i, fo2s[0]:fo2s[-1] + 1],
                                               len(fo2s) * DI)
            for j, fo2 in enumerate(fo2s):
                for ch in range(NCH):
                    nm = (fo2 * 2 + ch) % 4
                    ps = unit_ps(nm)
                    g = mm_group(nm, ps, slot * SLOTW + j * DI, NF_G,
                                 lambda kk, ch=ch: gs[:, (kk * 2 + ch) * CH:(kk * 2 + ch + 1) * CH],
                                 wtgt, pre_waits=((s_ve, inter_done),))
                    pl.wait("v", s_mm, g)
                    ev = pl.op("v", s_ve, lambda e, fo2=fo2, ps=ps, ch=ch:
                               e.tensor_tensor(accsl(fo2, ch), accsl(fo2, ch), ps, ALU.add))
                    st["evtgt"][nm] = (s_ve, ev)
            finish_group(si, slot, off, len(fo2s) * DI, g, wtgt)

    sgw, suw, sdw = din["sgw"], din["suw"], din["sdw"]
    for kh in range(2):
        for fos in chunks_of(list(range(11)), 3):
            fo_g = [kh * 11 + f for f in fos]
            si, slot, off, wtgt = stream_group(sgw[fo_g[0]:fo_g[-1] + 1], len(fos) * D)
            for j, fo in enumerate(fos):
                for ch in range(NCH):
                    nm = (fo * 2 + ch) % 4
                    ps = unit_ps(nm)
                    g = mm_group(nm, ps, slot * SLOTW + j * D, NKT,
                                 lambda kk, ch=ch: xnsl(kk, ch), wtgt)
                    pl.wait("s", s_mm, g)
                    ev = pl.op("s", s_sc, lambda e, fo=fo, ch=ch, ps=ps:
                               e.activation(gs[:, (fo * 2 + ch) * CH:(fo * 2 + ch + 1) * CH],
                                            ps, AF.Silu))
                    st["evtgt"][nm] = (s_sc, ev)
            finish_group(si, slot, off, len(fos) * D, g, wtgt)
        for fos in chunks_of(list(range(11)), 3):
            fo_g = [kh * 11 + f for f in fos]
            si, slot, off, wtgt = stream_group(suw[fo_g[0]:fo_g[-1] + 1], len(fos) * D)
            for j, fo in enumerate(fos):
                for ch in range(NCH):
                    nm = (fo * 2 + ch) % 4
                    ps = unit_ps(nm)
                    g = mm_group(nm, ps, slot * SLOTW + j * D, NKT,
                                 lambda kk, ch=ch: xnsl(kk, ch), wtgt)
                    ur = urot[0] % 4
                    urot[0] += 1
                    pl.wait("s", s_mm, g)
                    pl.wait("s", s_ve, st.get("us%d" % ur, 0))
                    ev = pl.op("s", s_sc, lambda e, ps=ps, ur=ur:
                               e.activation(us[:, ur * CH:(ur + 1) * CH], ps,
                                            AF.Copy, scale=0.5))
                    st["evtgt"][nm] = (s_sc, ev)
                    pl.wait("v", s_sc, ev)
                    uvt = pl.op("v", s_ve, lambda e, fo=fo, ch=ch, ur=ur:
                                e.tensor_tensor(gs[:, (fo * 2 + ch) * CH:(fo * 2 + ch + 1) * CH],
                                                gs[:, (fo * 2 + ch) * CH:(fo * 2 + ch + 1) * CH],
                                                us[:, ur * CH:(ur + 1) * CH], ALU.mult))
                    st["us%d" % ur] = uvt
            finish_group(si, slot, off, len(fos) * D, g, wtgt)
        sh_inter = env["s_ve"].n
        for fo2s in chunks_of(list(range(16)), 3):
            srcv = sdw[fo2s[0]:fo2s[-1] + 1, :, kh * 1408:(kh + 1) * 1408]
            si, slot, off, wtgt = stream_group(srcv.rearrange("f p t -> f p t") if False
                                               else srcv, len(fo2s) * 1408)
            for j, fo2 in enumerate(fo2s):
                for ch in range(NCH):
                    nm = (fo2 * 2 + ch) % 4
                    ps = unit_ps(nm)
                    g = mm_group(nm, ps, slot * SLOTW + j * 1408, 11,
                                 lambda kk, ch=ch: gs[:, (kk * 2 + ch) * CH:(kk * 2 + ch + 1) * CH],
                                 wtgt, pre_waits=((s_ve, sh_inter),))
                    pl.wait("v", s_mm, g)
                    ev = pl.op("v", s_ve, lambda e, fo2=fo2, ps=ps, ch=ch:
                               e.tensor_tensor(accsl(fo2, ch), accsl(fo2, ch), ps, ALU.add))
                    st["evtgt"][nm] = (s_ve, ev)
            finish_group(si, slot, off, len(fo2s) * 1408, g, wtgt)
            if kh == 1:
                f0, f1 = fo2s[0], fo2s[-1] + 1
                dstap = dst[f0:f1, :, t0:t0 + THTOK].rearrange("f p t -> p f t")
                srcap = acc[:].rearrange("p (f t) -> p f t", f=NKT)[:, f0:f1, :]
                pl.wait("q", s_ve, s_ve.n)
                pl.op("q", s_xq, lambda e, d=dstap, s=srcap: e.dma_start(out=d, in_=s))


def _routing_window(env, pl, rtile, sc_ap, comb_ap, base):
    P = env["P"]
    s_ve = env["s_ve"]
    s4c = rtile[:, base:base + 16]
    gsc = rtile[:, base + 16:base + 20]
    thr = rtile[:, base + 20:base + 21]
    gm = rtile[:, base + 21:base + 25]
    sm = rtile[:, base + 32:base + 48]
    t8 = rtile[:, base + 48:base + 56]
    selc = rtile[:, base + 64:base + 80]
    tmp4 = rtile[:, base + 80:base + 84]
    tmp1 = rtile[:, base + 84:base + 85]
    rs = rtile[:, base + 85:base + 86]

    def V(fn):
        pl.wait("v", s_ve, s_ve.n)
        pl.op("v", s_ve, fn)

    V(lambda e: e.tensor_tensor(s4c, sc_ap, P["bias_bc"][:], ALU.add))
    s4cv = s4c.rearrange("p (g j) -> p g j", j=4)
    slots = [s4cv[:, :, j] for j in range(4)]
    pairs = [(0, 1), (0, 2), (0, 3), (1, 2), (1, 3), (2, 3)]
    for n, (i, j) in enumerate(pairs):
        V(lambda e, i=i, j=j: e.tensor_tensor(tmp4, slots[i], slots[j], ALU.add))
        if n == 0:
            V(lambda e: e.tensor_copy(gsc, tmp4))
        else:
            V(lambda e: e.tensor_tensor(gsc, gsc, tmp4, ALU.max))
    gcol = [gsc[:, j:j + 1] for j in range(4)]
    for n, (i, j) in enumerate(pairs):
        V(lambda e, i=i, j=j: e.tensor_tensor(tmp1, gcol[i], gcol[j], ALU.min))
        if n == 0:
            V(lambda e: e.tensor_copy(thr, tmp1))
        else:
            V(lambda e: e.tensor_tensor(thr, thr, tmp1, ALU.max))
    V(lambda e: e.tensor_scalar(gm, gsc, thr, None, ALU.is_ge))
    smv = sm.rearrange("p (g j) -> p g j", j=4)
    for g in range(4):
        V(lambda e, g=g: e.tensor_copy(smv[:, g, :], gm[:, g:g + 1].to_broadcast([128, 4])))
    V(lambda e: e.tensor_tensor(sm, s4c, sm, ALU.mult))
    V(lambda e: e.tensor_tensor(sm, sm, P["idx_eps"][:], ALU.subtract))
    V(lambda e: e.max(out=t8, in_=sm))
    V(lambda e: e.tensor_scalar(selc, sm, t8[:, 5:6], None, ALU.is_ge))
    V(lambda e: e.tensor_tensor(comb_ap, sc_ap, selc, ALU.mult))
    V(lambda e: e.reduce_sum(rs, comb_ap, axis=AX.X))
    V(lambda e: e.tensor_scalar_add(rs, rs, 1e-20))
    V(lambda e: e.reciprocal(rs, rs))
    V(lambda e: e.tensor_scalar_mul(rs, rs, 1.25))
    V(lambda e: e.tensor_scalar(comb_ap, comb_ap, rs, None, ALU.mult))


def _xch_kr(x, h, pos=0):
    return x[pos * 3072 + h * 64: pos * 3072 + h * 64 + 64, :]


def _xch_vr(x, h, pos=0):
    return x[pos * 3072 + 1024 + h * 64: pos * 3072 + 1024 + h * 64 + 64, :]


def _xch_tail(x, h, which, pos=0):
    base = pos * 3072 + (2048 if which == "k" else 2560)
    v = x[base + h * 32: base + h * 32 + 32, :]
    return v.rearrange("w n -> (w n)").rearrange("(p t) -> p t", t=512)


def _emit_ln(env, pl, B, src_of, dst_of, pga, pgb):
    P = env["P"]
    s_mm, s_sc, s_ve = env["s_mm"], env["s_sc"], env["s_ve"]
    pstat, xh2, xsq2, rows, t32 = B["pstat"], B["xh2"], B["xsq2"], B["rows"], B["t32"]
    pl.wait("t", s_ve, s_ve.n)
    last = 0
    for k in range(NKT):
        b = k % 2
        if k >= 2:
            pl.wait("s", s_mm, last)
            pl.wait("v", s_mm, last)
        cp_t = pl.op("s", s_sc, lambda e, o=xh2[:, b * CH:(b + 1) * CH], i=src_of(k):
                     e.copy(o, i))
        sq_t = pl.op("v", s_ve, lambda e, o=xsq2[:, b * CH:(b + 1) * CH], i=src_of(k):
                     e.tensor_tensor(o, i, i, ALU.mult))
        pl.wait("t", s_sc, cp_t)
        pl.wait("t", s_ve, sq_t)
        pl.add("t", lambda e, m=xh2[:, b * CH:(b + 1) * CH], k=k:
               e.matmul(pstat[0:1, 0:CH], P["ones_col_bf"][:], m,
                        start=(k == 0), stop=(k == NKT - 1)))
        last = pl.op("t", s_mm, lambda e, m=xsq2[:, b * CH:(b + 1) * CH], k=k:
                     e.matmul(pstat[0:1, CH:2 * CH], P["ones_col_bf"][:], m,
                              start=(k == 0), stop=(k == NKT - 1)))
    mean_r = rows[0:1, 0:CH]
    var_r = rows[0:1, CH:2 * CH]
    aux_r = rows[0:1, 2 * CH:3 * CH]
    pl.wait("v", s_mm, last)
    pl.op("v", s_ve, lambda e: e.tensor_scalar_mul(mean_r, pstat[0:1, 0:CH], 1.0 / D))
    pl.op("v", s_ve, lambda e: e.tensor_scalar_mul(var_r, pstat[0:1, CH:2 * CH], 1.0 / D))
    pl.op("v", s_ve, lambda e: e.tensor_tensor(aux_r, mean_r, mean_r, ALU.mult))
    pl.op("v", s_ve, lambda e: e.tensor_tensor(var_r, var_r, aux_r, ALU.subtract))
    vt = pl.op("v", s_ve, lambda e: e.tensor_scalar_add(var_r, var_r, 1e-5))
    pl.wait("s", s_ve, vt)
    sct = pl.op("s", s_sc, lambda e: e.activation(aux_r, var_r, AF.Sqrt))
    pl.wait("v", s_sc, sct)
    vt = pl.op("v", s_ve, lambda e: e.reciprocal(var_r, aux_r))
    pl.wait("t", s_ve, vt)
    pl.wait("t", s_sc, s_sc.n)
    pl.add("t", lambda e: e.matmul(pga, P["ones_row_f32"][:], mean_r, start=True, stop=True))
    bc = pl.op("t", s_mm, lambda e: e.matmul(pgb, P["ones_row_f32"][:], var_r,
                                             start=True, stop=True))
    pl.wait("v", s_mm, bc)
    for k in range(NKT):
        pl.op("v", s_ve, lambda e, k=k: e.tensor_tensor(t32[:], src_of(k), pga, ALU.subtract))
        pl.op("v", s_ve, lambda e, k=k: e.tensor_tensor(dst_of(k), t32[:], pgb, ALU.mult))


def phase_qkv(env):
    nc, P, din = env["nc"], env["P"], env["din"]
    s_wg, s_xq, s_mm, s_sc, s_ve = env["s_wg"], env["s_xq"], env["s_mm"], env["s_sc"], env["s_ve"]
    y1d, qd, kd, vd = env["y1d"], env["qd"], env["kd"], env["vd"]
    xcho = env["xcho"].ap()
    NSL = 4
    KSCL = float(1.0 / np.sqrt(np.float64(128.0)))

    al = Alloc(nc)
    xn2 = al.sb("a_xn2", (128, NKT * TOK), BF16)
    y1c = al.sb("a_y1c", (128, 2 * NKT * CH), F32)
    wring = al.sb("a_wring", (128, NSL * D), BF16)
    cost = al.sb("a_cos", (128, TOK), F32)
    sinm = al.sb("a_sinm", (128, TOK), F32)
    qs = al.sb("a_qs", (128, 2 * CH), BF16)
    ropeb = al.sb("a_rope", (128, 2 * CH), BF16)
    t32a = al.sb("a_t32a", (128, CH), F32)
    t32b = al.sb("a_t32b", (128, CH), F32)
    r64 = al.sb("a_r64", (64, 2 * CH), BF16)
    vts = al.sb("a_vts", (128, CH), BF16)
    hinfo = al.sb("a_hinfo", (128, 4 * CH), BF16)
    xh2 = al.sb("a_xh", (128, 2 * CH), BF16)
    xsq2 = al.sb("a_xsq", (128, 2 * CH), BF16)
    rows = al.sb("a_rows", (1, 3 * CH), F32)
    grow = al.sb("a_grow", (1, TOK), F32)
    pstat = al.ps("a_pstat", (128, 2 * CH))
    pbig = al.ps("a_pbig", (128, 2 * CH))
    prot = al.ps("a_prot", (128, CH))
    pr64 = al.ps("a_pr64", (64, CH))
    pinf = al.ps("a_pinf", (1, CH))
    ptv = al.ps("a_ptv", (128, 2 * CH), BF16)
    if True:
        pl = Plans(env)
        pga, pgb = pbig[:, 0:CH], pbig[:, CH:2 * CH]
        B = dict(pstat=pstat, xh2=xh2, xsq2=xsq2, rows=rows, t32=t32a)

        pl.op("q", s_xq, lambda e: e.dma_start(out=cost[:], in_=din["ropc"]))
        pl.op("q", s_xq, lambda e: e.dma_start(out=sinm[:], in_=din["rops"]))

        y1ck = {}
        xn_ch = {}
        for ch in range(4):
            sl = ch % 2
            srcap = y1d[:, :, ch * CH:(ch + 1) * CH].rearrange("f p t -> p f t")
            dstap = y1c[:, sl * NKT * CH:(sl + 1) * NKT * CH].rearrange(
                "p (f t) -> p f t", f=NKT)
            for sem, val in y1ck.get(ch - 2, []):
                pl.wait("q", sem, val)
            ldt = pl.op("q", s_xq, lambda e, d=dstap, s=srcap: e.dma_start(out=d, in_=s))
            pl.wait("s", s_xq, ldt)
            pl.wait("v", s_xq, ldt)
            _emit_ln(env, pl, B,
                     src_of=lambda k, sl=sl: y1c[:, sl * NKT * CH + k * CH:
                                                 sl * NKT * CH + (k + 1) * CH],
                     dst_of=lambda k, ch=ch: xn2[:, k * TOK + ch * CH: k * TOK + ch * CH + CH],
                     pga=pstat[:, 0:CH], pgb=pstat[:, CH:2 * CH])
            y1ck[ch] = [(s_ve, s_ve.n), (s_sc, s_sc.n)]
            xn_ch[ch] = s_ve.n
        xn_done = s_ve.n

        st = dict(strip=0, strip_mm={}, evtgt={}, qs_dma=0, rope_dma=0,
                  pr64_mm=0, rope_mm=0)

        def stream_strip(srcap):
            si = st["strip"]
            slot = si % NSL
            if si >= NSL:
                pl.wait("g", s_mm, st["strip_mm"][si - NSL])
            tgt = pl.op("g", s_wg, lambda e, d=wring[:, slot * D: slot * D + srcap.shape[-1]],
                        s=srcap: e.dma_start(out=d, in_=s))
            st["strip"] += 1
            return si, slot, tgt

        def psum_wait(nm):
            if nm in st["evtgt"]:
                pl.wait("t", *st["evtgt"][nm])

        def qkv_group(slot, ps, ch):
            for kk in range(NKT):
                fn = lambda e, kk=kk, w0=slot * D, ps=ps, ch=ch: e.matmul(
                    ps, wring[:, w0 + kk * 128: w0 + (kk + 1) * 128],
                    xn2[:, kk * TOK + ch * CH: kk * TOK + ch * CH + CH],
                    start=(kk == 0), stop=(kk == NKT - 1))
                if kk == NKT - 1:
                    g = pl.op("t", s_mm, fn)
                else:
                    pl.add("t", fn)
            return g

        def emit_tail(kind, h, ch, ev, sl):
            qsl = qs[:, sl * CH:(sl + 1) * CH]
            rsl = ropeb[:, sl * CH:(sl + 1) * CH]
            r64l = r64[:, sl * CH:(sl + 1) * CH]
            if kind == "v":
                pl.wait("t", s_sc, ev)
                pl.wait("t", s_ve, st.get("ptv_ve", 0))
                for i4 in range(4):
                    fn = lambda e, i4=i4, qsl=qsl: e.transpose(
                        ptv[:, i4 * 128:(i4 + 1) * 128],
                        qsl[:, i4 * 128:(i4 + 1) * 128], P["id_bf"][:])
                    if i4 == 3:
                        tmm = pl.op("t", s_mm, fn)
                    else:
                        pl.add("t", fn)
                pl.wait("v", s_mm, tmm)
                pl.wait("v", s_xq, st.get("vts_dma", 0))
                vev = pl.op("v", s_ve, lambda e: e.tensor_copy(vts[:], ptv[:, 0:CH]))
                st["ptv_ve"] = vev
                pl.wait("q", s_ve, vev)
                st["vts_dma"] = pl.op("q", s_xq, lambda e, h=h, ch=ch:
                                      e.dma_start(out=env["vTd"][h][:, ch * CH:(ch + 1) * CH],
                                                  in_=vts[:]))
                if ch == 3:
                    st["vts_dma"] = pl.op("q", s_xq, lambda e, h=h:
                                          e.dma_start(out=_xch_tail(xcho, h, "v"), in_=vts[:]))
                psum_wait("pr64")
                g2 = pl.op("t", s_mm, lambda e, qsl=qsl: e.matmul(
                    pr64[:], P["pvT"][:], qsl, start=True, stop=True))
                st["pr64_mm%d" % sl] = g2
                pl.wait("s", s_mm, g2)
                pl.wait("s", s_xq, st.get("r64_dma%d" % sl, 0))
                e2 = pl.op("s", s_sc, lambda e, r64l=r64l: e.copy(r64l, pr64[:]))
                st["evtgt"]["pr64"] = (s_sc, e2)
                pl.wait("q", s_sc, e2)
                st["r64_dma%d" % sl] = pl.op("q", s_xq, lambda e, h=h, ch=ch, r64l=r64l:
                                             e.dma_start(out=_xch_vr(xcho, h)[:, ch * CH:(ch + 1) * CH],
                                                         in_=r64l))
            else:
                pl.wait("t", s_sc, ev)
                psum_wait("prot")
                rmm = pl.op("t", s_mm, lambda e, qsl=qsl: e.matmul(
                    prot[:], P["perm"][:], qsl, start=True, stop=True))
                pl.wait("v", s_mm, rmm)
                pl.wait("v", s_xq, st.get("rope_dma%d" % sl, 0))
                pl.wait("v", s_mm, st.get("rope_mm%d" % sl, 0))
                st["qs_ve%d" % sl] = pl.op("v", s_ve, lambda e, ch=ch, qsl=qsl:
                      e.tensor_tensor(t32a[:], qsl, cost[:, ch * CH:(ch + 1) * CH],
                                      ALU.mult))
                vt = pl.op("v", s_ve, lambda e, ch=ch:
                           e.tensor_tensor(t32b[:], prot[:], sinm[:, ch * CH:(ch + 1) * CH],
                                           ALU.mult))
                st["evtgt"]["prot"] = (s_ve, vt)
                vt = pl.op("v", s_ve, lambda e, rsl=rsl: e.tensor_tensor(rsl, t32a[:], t32b[:],
                                                                         ALU.add))
                dst0 = qd if kind == "q" else kd
                pl.wait("q", s_ve, vt)
                st["rope_dma%d" % sl] = pl.op("q", s_xq, lambda e, d=dst0, h=h, ch=ch, rsl=rsl:
                                              e.dma_start(out=d[h, :, ch * CH:(ch + 1) * CH],
                                                          in_=rsl))
                if kind == "k" and ch == 3:
                    st["rope_dma%d" % sl] = pl.op("q", s_xq, lambda e, h=h, rsl=rsl:
                                                  e.dma_start(out=_xch_tail(xcho, h, "k"),
                                                              in_=rsl))
                if kind == "k":
                    pl.wait("t", s_ve, vt)
                    psum_wait("pr64")
                    g2 = pl.op("t", s_mm, lambda e, rsl=rsl: e.matmul(
                        pr64[:], P["pkT"][:], rsl, start=True, stop=True))
                    st["rope_mm%d" % sl] = g2
                    pl.wait("s", s_mm, g2)
                    pl.wait("s", s_xq, st.get("r64_dma%d" % sl, 0))
                    e2 = pl.op("s", s_sc, lambda e, r64l=r64l: e.copy(r64l, pr64[:]))
                    st["evtgt"]["pr64"] = (s_sc, e2)
                    pl.wait("q", s_sc, e2)
                    st["r64_dma%d" % sl] = pl.op("q", s_xq, lambda e, h=h, ch=ch, r64l=r64l:
                                                 e.dma_start(
                                                     out=_xch_kr(xcho, h)[:, ch * CH:(ch + 1) * CH],
                                                     in_=r64l))

        pend_t = None
        for ft in range(48):
            si, slot, wtgt = stream_strip(din["wqkv"][ft])
            kind = "q" if ft < 16 else ("k" if ft < 32 else "v")
            h = ft % 16
            for ch in range(4):
                it = ft * 4 + ch
                sl = it % 2
                if ft == 0:
                    pl.wait("t", s_ve, xn_ch[ch])
                nm = "pga" if it % 2 == 0 else "pgb"
                ps = pga if nm == "pga" else pgb
                pl.wait("t", s_wg, wtgt)
                psum_wait(nm)
                g = qkv_group(slot, ps, ch)
                st["strip_mm"][si] = g
                pl.wait("s", s_mm, g)
                pl.wait("s", s_mm, st.get("pr64_mm%d" % sl, 0))
                pl.wait("s", s_ve, st.get("qs_ve%d" % sl, 0))
                if kind == "v":
                    ev = pl.op("s", s_sc, lambda e, ps=ps, sl=sl:
                               e.copy(qs[:, sl * CH:(sl + 1) * CH], ps))
                else:
                    scl = 1.0 if kind == "q" else KSCL
                    ev = pl.op("s", s_sc, lambda e, ps=ps, scl=scl, sl=sl:
                               e.activation(qs[:, sl * CH:(sl + 1) * CH], ps,
                                            AF.Copy, scale=scl))
                st["evtgt"][nm] = (s_sc, ev)
                if pend_t is not None:
                    emit_tail(*pend_t)
                pend_t = (kind, h, ch, ev, sl)
        emit_tail(*pend_t)

        pl.wait("g", s_xq, s_xq.n)
        pl.op("g", env["s_cc"], lambda e: e.collective_compute(
            "AllGather", ALU.bypass,
            replica_groups=[[0, 1], [2, 3], [4, 5], [6, 7]],
            ins=[env["xcho"].ap().opt()], outs=[env["xchi"].ap().opt()]),
            guard=False)

        for ch in range(4):
            for fi in range(4):
                si, slot, wtgt = stream_strip(din["iw1"][fi])
                nm = "pga" if fi % 2 == 0 else "pgb"
                ps = pga if nm == "pga" else pgb
                pl.wait("t", s_wg, wtgt)
                psum_wait(nm)
                g = qkv_group(slot, ps, ch)
                st["strip_mm"][si] = g
                pl.wait("s", s_mm, g)
                ev = pl.op("s", s_sc, lambda e, fi=fi, ps=ps:
                           e.activation(hinfo[:, fi * CH:(fi + 1) * CH], ps, AF.Gelu))
                st["evtgt"][nm] = (s_sc, ev)
            pl.wait("t", s_sc, s_sc.n)
            pl.wait("t", s_ve, s_ve.n)
            for kk in range(4):
                fn = lambda e, kk=kk: e.matmul(pinf[:], P["iw2"][:, kk:kk + 1],
                                               hinfo[:, kk * CH:(kk + 1) * CH],
                                               start=(kk == 0), stop=(kk == 3))
                if kk == 3:
                    g = pl.op("t", s_mm, fn)
                else:
                    pl.add("t", fn)
            pl.wait("s", s_mm, g)
            sct = pl.op("s", s_sc, lambda e:
                        e.activation(rows[0:1, 0:CH], pinf[0:1, :], AF.Sigmoid))
            pl.wait("v", s_sc, sct)
            pl.op("v", s_ve, lambda e, ch=ch:
                  e.tensor_single_scalar(grow[0:1, ch * CH:(ch + 1) * CH],
                                         rows[0:1, 0:CH], 0.75, ALU.is_gt))
        if DBG_ATTN:
            pl.wait("q", s_ve, s_ve.n)
            pl.op("q", s_xq, lambda e: e.dma_start(out=din["_dbggate"], in_=grow[:]))
        for ch in range(4):
            pl.wait("t", s_ve, s_ve.n)
            pl.wait("t", s_sc, s_sc.n)
            g = pl.op("t", s_mm, lambda e, ch=ch:
                      e.matmul(pga, P["ones_row_f32"][:], grow[0:1, ch * CH:(ch + 1) * CH],
                               start=True, stop=True))
            pl.wait("s", s_mm, g)
            pl.op("s", s_sc, lambda e, ch=ch:
                  e.copy(P["gate_bc"][:, ch * CH:(ch + 1) * CH], pga))

        pl.wait("q", s_xq, s_xq.n)
        pl.wait("q", s_sc, s_sc.n)
        pl.wait("g", s_wg, s_wg.n)
        pl.emit(nc, "qkv")
    al.close()


def phase_exchange(env):
    nc = env["nc"]
    s_cc = env["s_cc"]
    pl = Plans(env)

    def cc(e):
        return e.collective_compute(
            "AllGather", ALU.bypass,
            replica_groups=[[0, 1], [2, 3], [4, 5], [6, 7]],
            ins=[env["xcho"].ap().opt()], outs=[env["xchi"].ap().opt()])
    pl.op("g", s_cc, cc, guard=False)
    pl.wait("g", s_cc, s_cc.n)
    pl.emit(nc, "xch")


def phase_attn(env):
    nc, P, din = env["nc"], env["P"], env["din"]
    s_wg, s_xq, s_mm, s_sc, s_ve = env["s_wg"], env["s_xq"], env["s_mm"], env["s_sc"], env["s_ve"]
    y1d, y2d = env["y1d"], env["y2d"]
    NSL = 4

    al0 = Alloc(nc)
    ctxall = al0.sb("h_ctxall", (128, NH * TOK), BF16)
    lmask = al0.sb("h_lmask", (128, 5 * 640), BF16)
    if True:
        al = Alloc(nc)
        B = dict(ctxall=ctxall, lmask=lmask, hstate={},
                 reg_free={r: [] for r in range(4)},
                 pR_free={0: [], 1: []}, pG_free={0: [], 1: [], 2: []})
        for par in range(2):
            B["qh%d" % par] = al.sb("h_q%d" % par, (128, TOK), BF16)
            B["kloc%d" % par] = al.sb("h_kloc%d" % par, (128, 2560), BF16)
            B["vT%d" % par] = al.sb("h_vT%d" % par, (128, 2560), BF16)
            B["krT%d" % par] = al.sb("h_krT%d" % par, (64, 2 * TOK), BF16)
            B["vu%d" % par] = al.sb("h_vu%d" % par, (128, 2 * TOK), BF16)
            B["qrT%d" % par] = al.sb("h_qrT%d" % par, (64, TOK), BF16)
        B["vrT"] = al.sb("h_vrT", (64, 2 * TOK), BF16)
        B["probR"] = al.sb("h_probR", (128, 2 * 640), BF16)
        B["probG"] = al.sb("h_probG", (128, 3 * TOK), BF16)
        B["denP"] = al.sb("h_denP", (128, TOK), F32)
        B["ctxL"] = al.sb("h_ctxL", (128, TOK), BF16)
        B["t32"] = al.sb("h_t32", (128, 4 * CH), F32)
        B["t32b"] = al.sb("h_t32b", (128, CH), F32)
        B["rows"] = al.sb("h_rows", (1, TOK), F32)
        B["psA"] = al.ps("h_psA", (128, 2 * CH))
        B["psB"] = al.ps("h_psB", (128, 2 * CH))
        B["pctx"] = al.ps("h_pctx", (128, 4 * CH))
        if True:
            pl = Plans(env)
            pl.op("q", s_xq, lambda e: e.dma_start(out=lmask[:], in_=din["lmask"]))
            pl.wait("v", s_xq, s_xq.n)
            pl.wait("q", env["s_cc"], env["s_cc"].n)
            for h in range(NH):
                _attn_head(env, pl, h, B)
            pl.wait("q", s_xq, s_xq.n)
            pl.wait("g", s_wg, s_wg.n)
            pl.emit(nc, "heads")
        al.close()

        al = Alloc(nc)
        wring = al.sb("h_wring", (128, NSL * D), BF16)
        y1c = al.sb("h_y1c", (128, 2 * NKT * CH), F32)
        pw = al.ps("h_pw", (128, 2 * CH))
        if True:
            pl = Plans(env)
            st = dict(strip=0, strip_mm={}, evtgt={})
            stores = {}

            def issue_load(ch):
                if ch >= 4:
                    return 0
                sl = ch % 2
                src0 = y1d[:, :, ch * CH:(ch + 1) * CH].rearrange("f p t -> p f t")
                dst0 = y1c[:, sl * NKT * CH:(sl + 1) * NKT * CH].rearrange(
                    "p (f t) -> p f t", f=NKT)
                for sem, val in stores.get(ch - 2, []):
                    pl.wait("q", sem, val)
                return pl.op("q", s_xq, lambda e, d=dst0, s=src0:
                             e.dma_start(out=d, in_=s))

            lds = {0: issue_load(0), 1: issue_load(1)}
            for ch in range(4):
                sl = ch % 2
                y0 = sl * NKT * CH
                pl.wait("v", s_xq, lds[ch])
                for ft in range(16):
                    si = st["strip"]
                    slot = si % NSL
                    if si >= NSL:
                        pl.wait("g", s_mm, st["strip_mm"][si - NSL])
                    wtgt = pl.op("g", s_wg, lambda e, ft=ft, slot=slot:
                                 e.dma_start(out=wring[:, slot * D:(slot + 1) * D],
                                             in_=din["wo"][ft]))
                    st["strip"] += 1
                    nm = ft % 2
                    ps = pw[:, nm * CH:(nm + 1) * CH]
                    pl.wait("t", s_wg, wtgt)
                    if nm in st["evtgt"]:
                        pl.wait("t", *st["evtgt"][nm])
                    for kh in range(16):
                        fn = lambda e, kh=kh, slot=slot, ps=ps, ch=ch: e.matmul(
                            ps, wring[:, slot * D + kh * 128: slot * D + (kh + 1) * 128],
                            ctxall[:, kh * TOK + ch * CH: kh * TOK + ch * CH + CH],
                            start=(kh == 0), stop=(kh == 15))
                        if kh == 15:
                            g = pl.op("t", s_mm, fn)
                        else:
                            pl.add("t", fn)
                    st["strip_mm"][si] = g
                    pl.wait("v", s_mm, g)
                    ev = pl.op("v", s_ve, lambda e, ft=ft, ps=ps, y0=y0:
                               e.tensor_tensor(y1c[:, y0 + ft * CH:y0 + (ft + 1) * CH],
                                               y1c[:, y0 + ft * CH:y0 + (ft + 1) * CH],
                                               ps, ALU.add))
                    st["evtgt"][nm] = (s_ve, ev)
                pl.wait("q", s_ve, s_ve.n)
                dst1 = y2d[:, :, ch * CH:(ch + 1) * CH].rearrange("f p t -> p f t")
                src1 = y1c[:, y0:y0 + NKT * CH].rearrange("p (f t) -> p f t", f=NKT)
                stgt = pl.op("q", s_xq, lambda e, d=dst1, s=src1: e.dma_start(out=d, in_=s))
                stores[ch] = [(s_xq, stgt)]
                lds[ch + 2] = issue_load(ch + 2)
            pl.wait("q", s_xq, s_xq.n)
            pl.wait("g", s_wg, s_wg.n)
            pl.emit(nc, "wo")
        al.close()
    al0.close()


def _attn_head(env, pl, h, B):
    nc, P = env["nc"], env["P"]
    s_wg, s_xq, s_mm, s_sc, s_ve = (env["s_wg"], env["s_xq"], env["s_mm"],
                                    env["s_sc"], env["s_ve"])
    qd, kd, vd = env["qd"], env["kd"], env["vd"]
    xchi = env["xchi"].ap()
    par = h % 2
    qh, kloc, vT = B["qh%d" % par], B["kloc%d" % par], B["vT%d" % par]
    krT, vu, qrT = B["krT%d" % par], B["vu%d" % par], B["qrT%d" % par]
    vrT = B["vrT"]
    probR, probG, denP, ctxL = B["probR"], B["probG"], B["denP"], B["ctxL"]
    t32, t32b, rows = B["t32"], B["t32b"], B["rows"]
    psA, psB, pctx = B["psA"], B["psB"], B["pctx"]
    ctxall, lmask = B["ctxall"], B["lmask"]
    reg_free, pR_free, pG_free = B["reg_free"], B["pR_free"], B["pG_free"]

    def reg(r):
        t = psA if r % 2 == 0 else psB
        return t[:, (r // 2) * CH:(r // 2 + 1) * CH]

    if "vrT_mm" in B:
        pl.wait("q", s_mm, B["vrT_mm"])
    for d, s in [(vrT[:, 0:TOK], _xch_vr(xchi, h, 0)),
                 (vrT[:, TOK:2 * TOK], _xch_vr(xchi, h, 1))]:
        pl.op("q", s_xq, lambda e, d=d, s=s: e.dma_start(out=d, in_=s))
    ld_vr = s_xq.n
    for sem, val in B["hstate"].get(par, []):
        pl.wait("q", sem, val)
    for d, s in [
        (qh[:], qd[h]),
        (kloc[:, 512:2560], kd[h]),
        (kloc[:, 0:512], _xch_tail(xchi, h, "k", 0)),
        (krT[:, 0:TOK], _xch_kr(xchi, h, 0)),
        (krT[:, TOK:2 * TOK], _xch_kr(xchi, h, 1)),
    ]:
        pl.op("q", s_xq, lambda e, d=d, s=s: e.dma_start(out=d, in_=s))
    ld_k = s_xq.n
    pl.op("q", s_xq, lambda e: e.dma_start(
        out=vT[:, 0:512], in_=_xch_tail(xchi, h, "v", 0)))
    pl.op("q", s_xq, lambda e: e.dma_start(
        out=vT[:, 512:2560], in_=env["vTd"][h]))
    ld_vt = s_xq.n

    pl.wait("v", s_mm, B.get("denP_mm", 0))
    pl.op("v", s_ve, lambda e: e.memset(denP[:], 0.0))

    pl.wait("t", s_xq, ld_vr)
    for u in range(8):
        r = u % 4
        for sem, val in reg_free[r]:
            pl.wait("t", sem, val)
        for jj in range(4):
            kt = u * 4 + jj
            fn = lambda e, kt=kt, jj=jj, r=r: e.matmul(
                reg(r)[:, jj * 128:(jj + 1) * 128], vrT[:, kt * 128:(kt + 1) * 128],
                P["uoT"][:], start=True, stop=True)
            if jj == 3:
                g = pl.op("t", s_mm, fn)
            else:
                pl.add("t", fn)
        pl.wait("v", s_mm, g)
        cv = pl.op("v", s_ve, lambda e, u=u, r=r:
                   e.tensor_copy(vu[:, u * 512:(u + 1) * 512], reg(r)))
        reg_free[r] = [(s_ve, cv)]
    B["vrT_mm"] = s_mm.n

    pl.wait("t", s_xq, ld_k)
    for ch in range(4):
        r = ch % 4
        for sem, val in reg_free[r]:
            pl.wait("t", sem, val)
        g = pl.op("t", s_mm, lambda e, ch=ch, r=r: e.matmul(
            reg(r)[0:64, :], P["pkT"][:], qh[:, ch * CH:(ch + 1) * CH],
            start=True, stop=True))
        pl.wait("v", s_mm, g)
        qv = pl.op("v", s_ve, lambda e, ch=ch, r=r:
                   e.tensor_scalar_mul(qrT[:, ch * CH:(ch + 1) * CH], reg(r)[0:64, :], 2.0))
        reg_free[r] = [(s_ve, qv)]

    pl.wait("t", s_xq, ld_vt)
    for sem, val in B.get("pctx_free", []):
        pl.wait("t", sem, val)
    ctx_started = [False] * 4

    def issue_lctx(j, vmt):
        k0 = -512 + 128 * j
        lo, hi = max(0, k0), min(TOK, k0 + 640)
        pslot = probR[:, (j % 2) * 640: (j % 2) * 640 + 640]
        pl.wait("t", s_ve, vmt)
        ga = lo
        while ga < hi:
            gb = min(hi, (ga // CH + 1) * CH)
            bank = ga // CH
            stt = not ctx_started[bank]
            ctx_started[bank] = True
            gc = pl.op("t", s_mm, lambda e, j=j, ga=ga, gb=gb, stt=stt, k0=k0,
                       pslot=pslot: e.matmul(
                pctx[:, ga:gb], vT[:, j * 128:(j + 1) * 128],
                pslot[:, ga - k0:gb - k0], start=stt, stop=False))
            ga = gb
        return gc

    pend_l = None
    for j in range(20):
        k0 = -512 + 128 * j
        lo, hi = max(0, k0), min(TOK, k0 + 640)
        c0, c1 = lo - k0, hi - k0
        mbase = (j + 1) * 640 if j < 4 else 0
        unit = j % 2
        ps = psA if unit == 0 else psB
        regs = (0, 2) if unit == 0 else (1, 3)
        slot = j % 2
        pslot = probR[:, slot * 640: slot * 640 + 640]
        for r in regs:
            for sem, val in reg_free[r]:
                pl.wait("t", sem, val)
        a = c0
        while a < c1:
            b2 = min(c1, 512) if a < 512 else c1
            fn = lambda e, j=j, a=a, b2=b2, k0=k0, ps=ps: e.matmul(
                ps[:, a:b2], kloc[:, j * 128:(j + 1) * 128], qh[:, k0 + a:k0 + b2],
                start=True, stop=True)
            if b2 == c1:
                g = pl.op("t", s_mm, fn)
            else:
                pl.add("t", fn)
            a = b2
        pl.wait("s", s_mm, g)
        for sem, val in pR_free[slot]:
            pl.wait("s", sem, val)
        sct = pl.op("s", s_sc, lambda e, c0=c0, c1=c1, ps=ps, pslot=pslot:
                    e.activation(pslot[:, c0:c1], ps[:, c0:c1], AF.Exp))
        pl.wait("v", s_sc, sct)
        vmt = pl.op("v", s_ve, lambda e, c0=c0, c1=c1, mbase=mbase, pslot=pslot:
                    e.tensor_tensor(pslot[:, c0:c1], pslot[:, c0:c1],
                                    lmask[:, mbase:mbase + (c1 - c0)], ALU.mult))
        vda = pl.op("v", s_ve, lambda e, lo=lo, hi=hi, c0=c0, c1=c1, pslot=pslot:
                    e.tensor_tensor(denP[:, lo:hi], denP[:, lo:hi],
                                    pslot[:, c0:c1], ALU.add))
        if pend_l is not None:
            gc = issue_lctx(pend_l[0], pend_l[1])
            pR_free[pend_l[0] % 2] = [(s_ve, pend_l[2]), (s_mm, gc)]
        pend_l = (j, vmt, vda)
        for r in regs:
            reg_free[r] = [(s_sc, sct)]
    gc = issue_lctx(pend_l[0], pend_l[1])
    pR_free[pend_l[0] % 2] = [(s_ve, pend_l[2]), (s_mm, gc)]

    pl.wait("t", s_ve, s_ve.n)
    for r in range(4):
        for sem, val in reg_free[r]:
            pl.wait("t", sem, val)
    dred = {}
    for qq in range(4):
        dred[qq] = pl.op("t", s_mm, lambda e, qq=qq: e.matmul(
            reg(qq)[0:1, :], P["ones_col_f32"][:], denP[:, qq * CH:(qq + 1) * CH],
            start=True, stop=True))
    loc_red_mm = s_mm.n
    pl.wait("v", s_mm, loc_red_mm)
    pl.op("v", s_ve, lambda e: e.memset(denP[:], 0.0))
    l2 = {}
    for qq in range(4):
        pl.wait("s", s_mm, dred[qq])
        pl.op("s", s_sc, lambda e, qq=qq: e.activation(
            rows[0:1, qq * CH:(qq + 1) * CH], reg(qq)[0:1, :], AF.Ln))
        l2[qq] = pl.op("s", s_sc, lambda e, qq=qq: e.activation(
            rows[0:1, qq * CH:(qq + 1) * CH], rows[0:1, qq * CH:(qq + 1) * CH],
            AF.Exp, scale=-1.0))
    bgs = {}
    for qq in range(4):
        pl.wait("t", s_sc, l2[qq])
        bgs[qq] = pl.op("t", s_mm, lambda e, qq=qq: e.matmul(
            reg(qq)[:], P["ones_row_f32"][:], rows[0:1, qq * CH:(qq + 1) * CH],
            start=True, stop=True))
    pl.wait("s", s_ve, B.get("t32_free", 0))
    cps = {}
    for qq in range(4):
        pl.wait("s", s_mm, bgs[qq])
        cps[qq] = pl.op("s", s_sc, lambda e, qq=qq: e.copy(
            t32[:, qq * CH:(qq + 1) * CH], reg(qq)[:]))
        reg_free[qq] = [(s_sc, cps[qq])]
    for qq in range(4):
        pl.wait("v", s_sc, cps[qq])
        pl.op("v", s_ve, lambda e, qq=qq: e.tensor_tensor(
            ctxL[:, qq * CH:(qq + 1) * CH], pctx[:, qq * CH:(qq + 1) * CH],
            t32[:, qq * CH:(qq + 1) * CH], ALU.mult))
    ctxl_done = s_ve.n

    for r in range(4):
        for sem, val in reg_free[r]:
            pl.wait("t", sem, val)
    expA = expB = 0
    exps = {}
    pend = []

    def issue_ctx(kt):
        gsl = probG[:, (kt % 3) * TOK:(kt % 3) * TOK + TOK]
        ea, eb = exps[kt]
        if kt == 0:
            pl.wait("t", s_ve, ctxl_done)
        pl.wait("t", s_sc, ea)
        for qq in range(4):
            if qq == 2:
                pl.wait("t", s_sc, eb)
            fn = lambda e, kt=kt, qq=qq, gsl=gsl: e.matmul(
                pctx[:, qq * CH:(qq + 1) * CH], vu[:, kt * 128:(kt + 1) * 128],
                gsl[:, qq * CH:(qq + 1) * CH], start=(kt == 0), stop=(kt == 31))
            if qq == 3:
                g = pl.op("t", s_mm, fn)
            else:
                pl.add("t", fn)
        return g

    def flush_ctx():
        for (k2, db2) in pend:
            gc = issue_ctx(k2)
            pG_free[k2 % 3] = [(s_ve, db2), (s_mm, gc)]
        del pend[:]

    for kt in range(32):
        slot = kt % 3
        gsl = probG[:, slot * TOK: slot * TOK + TOK]
        if expA:
            pl.wait("t", s_sc, expA)
        for qq in (0, 1):
            fn = lambda e, kt=kt, qq=qq: e.matmul(
                psA[:, qq * CH:(qq + 1) * CH], krT[:, kt * 128:(kt + 1) * 128],
                qrT[:, qq * CH:(qq + 1) * CH], start=True, stop=True)
            if qq == 1:
                gA = pl.op("t", s_mm, fn)
            else:
                pl.add("t", fn)
        pl.wait("s", s_mm, gA)
        for sem, val in pG_free[slot]:
            pl.wait("s", sem, val)
        expA = pl.op("s", s_sc, lambda e, gsl=gsl:
                     e.activation(gsl[:, 0:2 * CH], psA[:], AF.Exp))
        if expB:
            pl.wait("t", s_sc, expB)
        for qq in (2, 3):
            fn = lambda e, kt=kt, qq=qq: e.matmul(
                psB[:, (qq - 2) * CH:(qq - 1) * CH], krT[:, kt * 128:(kt + 1) * 128],
                qrT[:, qq * CH:(qq + 1) * CH], start=True, stop=True)
            if qq == 3:
                gB = pl.op("t", s_mm, fn)
            else:
                pl.add("t", fn)
        pl.wait("s", s_mm, gB)
        expB = pl.op("s", s_sc, lambda e, gsl=gsl:
                     e.activation(gsl[:, 2 * CH:4 * CH], psB[:], AF.Exp))
        exps[kt] = (expA, expB)
        pl.wait("v", s_sc, expA)
        pl.op("v", s_ve, lambda e, gsl=gsl: e.tensor_tensor(
            denP[:, 0:2 * CH], denP[:, 0:2 * CH], gsl[:, 0:2 * CH], ALU.add))
        pl.wait("v", s_sc, expB)
        db = pl.op("v", s_ve, lambda e, gsl=gsl: e.tensor_tensor(
            denP[:, 2 * CH:4 * CH], denP[:, 2 * CH:4 * CH], gsl[:, 2 * CH:4 * CH],
            ALU.add))
        pend.append((kt, db))
        if len(pend) == 2:
            flush_ctx()
    flush_ctx()
    glob_mm = s_mm.n

    pl.wait("t", s_ve, s_ve.n)
    pl.wait("t", s_sc, expB)
    gred = {}
    for qq in range(4):
        gred[qq] = pl.op("t", s_mm, lambda e, qq=qq: e.matmul(
            reg(qq)[0:1, :], P["ones_col_f32"][:], denP[:, qq * CH:(qq + 1) * CH],
            start=True, stop=True))
    B["denP_mm"] = s_mm.n
    l2g = {}
    for qq in range(4):
        pl.wait("s", s_mm, gred[qq])
        pl.op("s", s_sc, lambda e, qq=qq: e.activation(
            rows[0:1, qq * CH:(qq + 1) * CH], reg(qq)[0:1, :], AF.Ln))
        l2g[qq] = pl.op("s", s_sc, lambda e, qq=qq: e.activation(
            rows[0:1, qq * CH:(qq + 1) * CH], rows[0:1, qq * CH:(qq + 1) * CH],
            AF.Exp, scale=-1.0))
    bgs = {}
    for qq in range(4):
        pl.wait("t", s_sc, l2g[qq])
        bgs[qq] = pl.op("t", s_mm, lambda e, qq=qq: e.matmul(
            reg(qq)[:], P["ones_row_f32"][:], rows[0:1, qq * CH:(qq + 1) * CH],
            start=True, stop=True))
    pl.wait("s", s_ve, ctxl_done)
    cps = {}
    for qq in range(4):
        pl.wait("s", s_mm, bgs[qq])
        cps[qq] = pl.op("s", s_sc, lambda e, qq=qq: e.copy(
            t32[:, qq * CH:(qq + 1) * CH], reg(qq)[:]))
        reg_free[qq] = [(s_sc, cps[qq])]
    pl.wait("v", s_mm, glob_mm)
    for qq in range(4):
        pl.wait("v", s_sc, cps[qq])
        pl.op("v", s_ve, lambda e, qq=qq: e.tensor_tensor(
            t32b[:], pctx[:, qq * CH:(qq + 1) * CH], t32[:, qq * CH:(qq + 1) * CH],
            ALU.mult))
        pl.op("v", s_ve, lambda e, qq=qq: e.tensor_tensor(
            t32b[:], t32b[:], P["gate_bc"][:, qq * CH:(qq + 1) * CH], ALU.mult))
        pl.op("v", s_ve, lambda e, qq=qq, h=h: e.tensor_tensor(
            ctxall[:, h * TOK + qq * CH: h * TOK + (qq + 1) * CH],
            t32b[:], ctxL[:, qq * CH:(qq + 1) * CH], ALU.add))
    B["t32_free"] = s_ve.n
    B["pctx_free"] = [(s_ve, s_ve.n)]
    B["hstate"][par] = [(s_mm, s_mm.n)]


def _tile_w(w):
    fo, fi = w.shape
    nft, nk = fo // 128, fi // 128
    return np.ascontiguousarray(
        w.T.reshape(nk, 128, nft, 128).transpose(2, 1, 0, 3).reshape(nft, 128, nk * 128)
    ).astype(np.float32)


def _rope_tables(pos0):
    a = np.arange(0, DH, 2, dtype=np.float32)
    e = a / np.float32(DH)
    b = np.float32(10000.0) ** e
    inv = np.float32(1.0) / b
    t = np.arange(T_FULL, dtype=np.float32)
    freqs = np.einsum("i,j->ij", t, inv).astype(np.float32)
    emb = np.concatenate([freqs, freqs], axis=-1)
    cos = np.cos(emb).astype(np.float32)
    sinm = np.sin(emb).astype(np.float32)
    sinm[:, 0::2] *= np.float32(-1.0)
    return (np.ascontiguousarray(cos[pos0:pos0 + TOK].T),
            np.ascontiguousarray(sinm[pos0:pos0 + TOK].T))


def _lmask(half):
    m = np.zeros((128, 5 * 640), dtype=BF)
    i = np.arange(128)[:, None]
    j = np.arange(640)[None, :]
    m[:, 0:640] = ((j - i >= 0) & (j - i <= 511)).astype(BF)
    if half == 1:
        for jt in range(4):
            w = 128 * (jt + 1)
            blk = (j[:, :w] <= i + 128 * jt - 1).astype(BF)
            m[:, (jt + 1) * 640:(jt + 1) * 640 + w] = blk
    return m


def _host_inputs(L0_attn_x, L0_attn_w_qkv, L0_attn_w_o, L0_attn_pk, L0_attn_pv,
                 L0_attn_uo, L0_attn_info_w1, L0_attn_info_w2, L0_ffn_gate_weight,
                 L0_ffn_e_score_correction_bias, L0_ffn_gate_proj, L0_ffn_up_proj,
                 L0_ffn_down_proj, L0_ffn_shared_gate_w, L0_ffn_shared_up_w,
                 L0_ffn_shared_down_w):
    f32 = lambda a: np.asarray(a, dtype=np.float32)
    shared = {}
    shared["wqkv"] = _tile_w(f32(L0_attn_w_qkv))
    shared["wo"] = _tile_w(f32(L0_attn_w_o))
    shared["gpw"] = np.stack([_tile_w(f32(L0_ffn_gate_proj[e])) for e in range(NEXP)])
    shared["upw"] = np.stack([_tile_w(f32(L0_ffn_up_proj[e])) for e in range(NEXP)])
    shared["dpw"] = np.stack([_tile_w(f32(L0_ffn_down_proj[e])) for e in range(NEXP)])
    shared["sgw"] = _tile_w(f32(L0_ffn_shared_gate_w))
    shared["suw"] = _tile_w(f32(L0_ffn_shared_up_w))
    shared["sdw"] = _tile_w(f32(L0_ffn_shared_down_w))
    shared["iw1"] = _tile_w(f32(L0_attn_info_w1))
    shared["iw2"] = np.ascontiguousarray(f32(L0_attn_info_w2).T.reshape(4, 128).T)
    gwT = f32(L0_ffn_gate_weight).T
    shared["gww"] = np.ascontiguousarray(
        gwT.reshape(NKT, 128, 16).transpose(1, 0, 2).reshape(128, NKT * 16))
    shared["pkT"] = np.ascontiguousarray(f32(L0_attn_pk).T)
    shared["pvT"] = np.ascontiguousarray(f32(L0_attn_pv).T)
    shared["uoT"] = np.ascontiguousarray(f32(L0_attn_uo).T)
    shared["bias16"] = f32(L0_ffn_e_score_correction_bias).reshape(1, 16)

    x = f32(L0_attn_x)
    in_maps = []
    for c in range(8):
        b, h = c // 2, c % 2
        m = dict(shared)
        xs = np.ascontiguousarray(x[b, h * TOK:(h + 1) * TOK].T)
        m["x_fm"] = np.ascontiguousarray(xs.reshape(NKT, 128, TOK))
        cc, ss = _rope_tables(h * TOK)
        m["ropc"] = cc
        m["rops"] = ss
        m["lmask"] = _lmask(h)
        in_maps.append(m)
    return in_maps


def _run(inputs, n_phase=6):
    in_maps = _host_inputs(**inputs)
    nc = build_program(n_phase=n_phase)
    return run_bass_kernel_spmd(nc, in_maps, core_ids=list(range(8)))


def kernel(**inputs):
    res = _run(inputs, 6)
    out = np.empty((4, T_FULL, D), dtype=np.float32)
    for c in range(8):
        b, h = c // 2, c % 2
        o = res.results[c]["out_fm"].reshape(D, TOK)
        out[b, h * TOK:(h + 1) * TOK] = o.T
    return out
